# version 13
# baseline (speedup 1.0000x reference)
import contextlib
import numpy as np
import ml_dtypes

import concourse.bass as bass
import concourse.mybir as mybir
from concourse.bass_utils import run_bass_kernel_spmd

F32 = mybir.dt.float32
BF16 = mybir.dt.bfloat16
AF = mybir.ActivationFunctionType
ALU = mybir.AluOpType
AX = mybir.AxisListType

NCORES = 8
N = 16384
D = 1024
TOWN = 2048
HALO = 640
TEXT = TOWN + HALO
NT = TEXT // 128
RING = 6
RC = TEXT + 1
HID = 2730
HIDP = 2816
NPAIR = 22
CCROWS = 512 + 8 + 5120
NEG = -1e30

ENGS = ("sync", "gpsimd", "vector", "scalar", "tensor")
ONE_INC = {"t", "v", "s", "g", "cc"}
SERIAL = [False]



def _prep_shared(inp):
    f32 = np.float32
    sh = {}
    sh["wqkvo_h"] = np.ascontiguousarray(inp["L0_attn_W_qkv"].T).reshape(8, 128, 3072)
    sh["wqkv_h"] = np.ascontiguousarray(inp["L0_ffn_wqkv"]).reshape(2, 8, 128, 1536)
    sh["wo_h"] = np.ascontiguousarray(inp["L0_ffn_wo"]).reshape(2, 4, 128, 1024)

    w1 = inp["L0_ffn_w1"]
    w1a = np.zeros((2, 1024, HIDP), f32); w1a[:, :, :HID] = w1[:, :, :HID]
    w1g = np.zeros((2, 1024, HIDP), f32); w1g[:, :, :HID] = w1[:, :, HID:]
    w1_h = np.zeros((2, NPAIR, 8, 128, 256), f32)
    for i in range(NPAIR):
        for k in range(8):
            w1_h[:, i, k, :, 0:128] = w1a[:, 128 * k:128 * (k + 1), 128 * i:128 * (i + 1)]
            w1_h[:, i, k, :, 128:256] = w1g[:, 128 * k:128 * (k + 1), 128 * i:128 * (i + 1)]
    sh["w1_h"] = w1_h

    w2 = inp["L0_ffn_w2"]
    w2p = np.zeros((2, HIDP, 1024), f32); w2p[:, :HID] = w2
    w2_h = np.zeros((2, 8, NPAIR, 128, 128), f32)
    for oc in range(8):
        for k in range(NPAIR):
            w2_h[:, oc, k] = w2p[:, 128 * k:128 * (k + 1), 128 * oc:128 * (oc + 1)]
    sh["w2_h"] = w2_h

    b1 = inp["L0_ffn_b1"]
    b1a = np.zeros((2, HIDP), f32); b1a[:, :HID] = b1[:, :HID]
    b1g = np.zeros((2, HIDP), f32); b1g[:, :HID] = b1[:, HID:]
    b1_h = np.zeros((2, 128, 2 * NPAIR), f32)
    for i in range(NPAIR):
        b1_h[:, :, 2 * i] = b1a[:, 128 * i:128 * (i + 1)]
        b1_h[:, :, 2 * i + 1] = b1g[:, 128 * i:128 * (i + 1)]
    sh["b1_h"] = b1_h
    sh["b2_h"] = np.ascontiguousarray(inp["L0_ffn_b2"].reshape(2, 8, 128).transpose(0, 2, 1))
    sh["lng_h"] = np.ascontiguousarray(inp["L0_ffn_ln_g"].reshape(2, 8, 128).transpose(0, 2, 1))
    sh["lnb_h"] = np.ascontiguousarray(inp["L0_ffn_ln_b"].reshape(2, 8, 128).transpose(0, 2, 1))
    sh["qs_d"] = np.ascontiguousarray(inp["L0_ffn_q_scale"])
    sh["ks_d"] = np.ascontiguousarray(inp["L0_ffn_k_scale"])
    return sh


def _rotary_tabs(t):
    f32 = np.float32
    half_idx = np.arange(0, 64, 2, dtype=f32)
    inv_freq = (1.0 / (f32(10000.0) ** (half_idx / f32(64.0)))).astype(f32)
    t = t.astype(f32)
    freqs = (t[:, None] * inv_freq[None, :]).astype(f32)
    base = ((half_idx + f32(0.4 * 64)) / f32(1.4 * 64)).astype(f32)
    power = ((t - f32(N // 2)) / f32(256.0)).astype(f32)
    scale = (base[None, :] ** power[:, None]).astype(f32)
    cos = np.cos(freqs).astype(f32)
    sin = np.sin(freqs).astype(f32)
    c64 = np.concatenate([cos, cos], 1)
    s64 = np.concatenate([sin, sin], 1)
    sc64 = np.concatenate([scale, scale], 1)
    sgn = np.ones(64, f32); sgn[:32] = -1.0
    tabs = np.empty((len(t), 256), f32)
    tabs[:, 0:64] = 8.0 * c64 * sc64
    tabs[:, 64:128] = sgn * 8.0 * s64 * sc64
    tabs[:, 128:192] = c64 / sc64
    tabs[:, 192:256] = sgn * s64 / sc64
    return tabs


def _masks_for(core):
    i = np.arange(128)[:, None]
    j = np.arange(640)[None, :]
    m0 = np.where((j >= i) & (j <= i + 512), 0.0, NEG).astype(np.float32)
    masks = np.stack([m0] * 5)
    if core == 0:
        for slot in range(1, 5):
            qb = 128 * (slot - 1)
            masks[slot] = np.where(j < 512 - qb, NEG, m0)
    return masks.astype(ml_dtypes.bfloat16)


def _prep_core(inp, sh, core):
    f32 = np.float32
    s = TOWN * core
    x = inp["L0_attn_x"].reshape(N, D)
    lo = s - HALO
    if lo >= 0:
        x_sl = np.ascontiguousarray(x[lo:s + TOWN])
    else:
        x_sl = np.concatenate([np.zeros((-lo, D), f32), x[0:s + TOWN]], 0)
    m = {"x_sl": x_sl,
         "tabs_d": _rotary_tabs(np.arange(lo, s + TOWN)),
         "masks_d": _masks_for(core),
         "bscale_d": np.array([[0.0 if core == 0 else 1.0]], f32)}
    m.update(sh)
    return m



class Emitter:
    def __init__(self, nc, es):
        self.nc = nc
        self.streams = {e: [] for e in ENGS}
        self.sems = {}
        self.counts = {}
        self.dma_sems = ("d0", "d1", "d2", "d3", "wd0", "wd1", "wd2", "bd", "od0", "od1",
                         "gd", "go0", "go1", "gx0", "gx1")
        for name in ("t", "v", "s", "g", "cc") + self.dma_sems:
            self.sems[name] = es.enter_context(nc.semaphore("sem_" + name))
            self.counts[name] = 0
        self._lastw = {}

    def op(self, eng, fn, waits=(), inc=None):
        if SERIAL[0]:
            waits = [(s, v) for s, v in self.counts.items() if v > 0]
        tok = None
        if inc is not None:
            self.counts[inc] += 1 if inc in ONE_INC else 16
            tok = (inc, self.counts[inc])
        waits = tuple((s, v) for t_ in waits if t_ is not None for (s, v) in (t_,) if v > 0)
        sems = self.sems

        def thunk(e, waits=waits, fn=fn, inc=inc, eng=eng, self=self):
            for s, v in waits:
                key = (eng, s)
                if self._lastw.get(key, -1) < v:
                    e.wait_ge(sems[s], v)
                    self._lastw[key] = v
            inst = fn(e)
            if inc is not None:
                inst.then_inc(sems[inc], 1 if inc in ONE_INC else 16)
            return inst

        self.streams[eng].append(thunk)
        return tok

    def run_block(self, name):
        for sname in self.dma_sems:
            v = self.counts[sname]
            if v > 0:
                self.streams["sync"].append(
                    lambda e, s=self.sems[sname], v=v: e.wait_ge(s, v))
        nc = self.nc
        streams = self.streams
        self._lastw = {}
        with nc.Block(name) as blk:
            for eng in ENGS:
                if not streams[eng]:
                    continue

                def make(eng):
                    thunks = streams[eng]

                    def body(e):
                        for th in thunks:
                            th(e)
                    return body
                getattr(blk, eng)(make(eng))
        self.streams = {e: [] for e in ENGS}



def build_program(n_phases=99, debug=False):
    nc = bass.Bass("TRN2", target_bir_lowering=False)

    x_sl = nc.dram_tensor("x_sl", [TEXT, D], F32, kind="ExternalInput")
    tabs_d = nc.dram_tensor("tabs_d", [TEXT, 256], F32, kind="ExternalInput")
    masks_d = nc.dram_tensor("masks_d", [5, 128, 640], BF16, kind="ExternalInput")
    bscale_d = nc.dram_tensor("bscale_d", [1, 1], F32, kind="ExternalInput")
    wqkvo_h = nc.dram_tensor("wqkvo_h", [8, 128, 3072], F32, kind="ExternalInput")
    wqkv_h = nc.dram_tensor("wqkv_h", [2, 8, 128, 1536], F32, kind="ExternalInput")
    wo_h = nc.dram_tensor("wo_h", [2, 4, 128, 1024], F32, kind="ExternalInput")
    w1_h = nc.dram_tensor("w1_h", [2, NPAIR, 8, 128, 256], F32, kind="ExternalInput")
    w2_h = nc.dram_tensor("w2_h", [2, 8, NPAIR, 128, 128], F32, kind="ExternalInput")
    b1_h = nc.dram_tensor("b1_h", [2, 128, 2 * NPAIR], F32, kind="ExternalInput")
    b2_h = nc.dram_tensor("b2_h", [2, 128, 8], F32, kind="ExternalInput")
    lng_h = nc.dram_tensor("lng_h", [2, 128, 8], F32, kind="ExternalInput")
    lnb_h = nc.dram_tensor("lnb_h", [2, 128, 8], F32, kind="ExternalInput")
    qs_d = nc.dram_tensor("qs_d", [2, 64], F32, kind="ExternalInput")
    ks_d = nc.dram_tensor("ks_d", [2, 64], F32, kind="ExternalInput")
    out_d = nc.dram_tensor("out_d", [TOWN, D], F32, kind="ExternalOutput")
    q_dram = nc.dram_tensor("q_dram", [TOWN, D], BF16)
    cc_in = nc.dram_tensor("cc_in", [CCROWS, 128], F32)
    cc_out = nc.dram_tensor("cc_out", [NCORES * CCROWS, 128], F32)
    if debug:
        dbg_res = nc.dram_tensor("dbg_res", [128, 8 * RC], F32, kind="ExternalOutput")
        dbg2 = nc.dram_tensor("dbg2", [TEXT, D], F32, kind="ExternalOutput")
        dbg3 = nc.dram_tensor("dbg3", [128, 16], F32, kind="ExternalOutput")
        dbg4 = nc.dram_tensor("dbg4", [128, 2 * 2048 + 64], F32, kind="ExternalOutput")
        dbg_kv = nc.dram_tensor("dbg_kv", [64, 1024 + 16], F32, kind="ExternalOutput")
        dbg_q = nc.dram_tensor("dbg_q", [TOWN, D], BF16, kind="ExternalOutput")

    es = contextlib.ExitStack()
    RES = es.enter_context(nc.sbuf_tensor("RES", [128, 8 * RC], F32))
    masks = es.enter_context(nc.sbuf_tensor("masks", [128, 5 * 640], BF16))
    ident_b = es.enter_context(nc.sbuf_tensor("ident_b", [128, 128], BF16))
    ident_f = es.enter_context(nc.sbuf_tensor("ident_f", [128, 128], F32))
    ones_col_b = es.enter_context(nc.sbuf_tensor("ones_col_b", [128, 1], BF16))
    ones_row_f = es.enter_context(nc.sbuf_tensor("ones_row_f", [1, 128], F32))
    qs_rep = es.enter_context(nc.sbuf_tensor("qs_rep", [128, 128], F32))
    ks_rep = es.enter_context(nc.sbuf_tensor("ks_rep", [128, 128], F32))
    bsc_rep = es.enter_context(nc.sbuf_tensor("bsc_rep", [128, 1], F32))
    kv_part = es.enter_context(nc.sbuf_tensor("kv_part", [64, 1024], F32))
    ksum_part = es.enter_context(nc.sbuf_tensor("ksum_part", [64, 16], F32))
    kvaug = es.enter_context(nc.sbuf_tensor("kvaug", [128, 16 * 65], BF16))
    b1_sb = es.enter_context(nc.sbuf_tensor("b1_sb", [128, 2 * NPAIR], F32))
    b2_sb = es.enter_context(nc.sbuf_tensor("b2_sb", [128, 8], F32))
    lng_sb = es.enter_context(nc.sbuf_tensor("lng_sb", [128, 8], F32))
    lnb_sb = es.enter_context(nc.sbuf_tensor("lnb_sb", [128, 8], F32))
    PSP = [es.enter_context(nc.psum_tensor(f"PSP{i}", [128, 1024], F32, bank=2 * i)) for i in range(4)]
    PS = [PSP[i // 2][:, 512 * (i % 2):512 * (i % 2) + 512] for i in range(8)]
    junk = es.enter_context(nc.sbuf_tensor("junk", [128, 2], F32))
    es_o = contextlib.ExitStack()
    wqkvo = es_o.enter_context(nc.sbuf_tensor("wqkvo", [128, 8 * 3072], BF16))

    em = Emitter(nc, es)
    _vprev = [None]

    def _span(ap):
        try:
            off = int(ap.offset)
        except Exception:
            return None
        hi = off
        for step, count in list(ap.ap):
            hi += abs(int(step)) * (int(count) - 1)
        return (ap.tensor.name, off, hi + 1)

    def _ovl(a, b):
        return a and b and a[0] == b[0] and a[1] < b[2] and b[1] < a[2]

    def _vhazard(out_ap, in_aps):
        prev = _vprev[0]
        if prev is not None and any(_ovl(_span(i), prev) for i in in_aps if i is not None):
            em.op("vector", lambda e: e.tensor_single_scalar(junk[:, 0:2], junk[:, 0:2], 1.0, ALU.mult))
        _vprev[0] = _span(out_ap)

    def finish(ret_es=True):
        es_o.close()
        if debug:
            em.op("sync", lambda e: e.dma_start(out=dbg_res.ap(), in_=RES[:, :]), inc="od0")
            em.op("sync", lambda e: e.dma_start(out=dbg_kv.ap()[:, 0:1024], in_=kv_part[0:64, :]), inc="od0")
            em.op("sync", lambda e: e.dma_start(out=dbg_kv.ap()[:, 1024:1040], in_=ksum_part[0:64, :]), inc="od0")
            if n_phases >= 2:
                em.op("sync", lambda e: e.dma_start(out=dbg_q.ap(), in_=q_dram.ap()), inc="od0")
            em.run_block("dbgdump")
        es.close()
        return nc

    def t_mm(out, lhsT, rhs, start, stop, waits=(), inc=None):
        return em.op("tensor", lambda e, o=out, l=lhsT, r=rhs, a=start, b=stop:
                     e.matmul(o, l, r, start=a, stop=b), waits, inc)

    def t_tr(out, in_, ident, waits=(), inc=None):
        return em.op("tensor", lambda e, o=out, i=in_, d=ident: e.transpose(o, i, d), waits, inc)

    def v_copy(out, in_, waits=(), inc=None):
        _vhazard(out, [in_])
        return em.op("vector", lambda e, o=out, i=in_: e.tensor_copy(o, i), waits, inc)

    def v_tt(op, out, a, b, waits=(), inc=None):
        _vhazard(out, [a, b])
        return em.op("vector", lambda e, o=out, x=a, y=b, p=op: e.tensor_tensor(o, x, y, p), waits, inc)

    def v_ts(op, out, in_, scalar, waits=(), inc=None):
        _vhazard(out, [in_, scalar if not isinstance(scalar, float) else None])
        return em.op("vector", lambda e, o=out, i=in_, s=scalar, p=op:
                     e.tensor_single_scalar(o, i, s, p), waits, inc)

    def v_stt(out, in0, scalar, in1, op0, op1, waits=(), inc=None):
        _vhazard(out, [in0, in1, scalar if not isinstance(scalar, float) else None])
        return em.op("vector", lambda e, o=out, i0=in0, s=scalar, i1=in1, p0=op0, p1=op1:
                     e.scalar_tensor_tensor(out=o, in0=i0, scalar=s, in1=i1, op0=p0, op1=p1), waits, inc)

    def v_red(op, out, in_, negate=False, waits=(), inc=None):
        _vhazard(out, [in_])
        return em.op("vector", lambda e, o=out, i=in_, p=op, n=negate:
                     e.tensor_reduce(o, i, AX.X, p, negate=n), waits, inc)

    def v_recip(out, in_, waits=(), inc=None):
        _vhazard(out, [in_])
        return em.op("vector", lambda e, o=out, i=in_: e.reciprocal(o, i), waits, inc)

    def v_memset(ap, val, waits=(), inc=None):
        return em.op("vector", lambda e, a=ap, v=val: e.memset(a, v), waits, inc)

    def s_gap():
        em.op("scalar", lambda e: e.activation(out=junk[:, 0:2], in_=junk[:, 0:2], func=AF.Copy))

    def s_act(out, in_, func, bias=0.0, scale=1.0, accum=None, waits=(), inc=None):
        return em.op("scalar", lambda e, o=out, i=in_, f=func, b=bias, s=scale, a=accum:
                     e.activation(out=o, in_=i, func=f, bias=b, scale=s, accum_out=a), waits, inc)

    def dma(eng, out, in_, waits=(), sem="d0"):
        return em.op(eng, lambda e, o=out, i=in_: e.dma_start(out=o, in_=i), waits, inc=sem)

    def res_ap(k, i0, ncols):
        return RES[:, RC * k + 1 + i0: RC * k + 1 + i0 + ncols]

    def ps_bf(i):
        b = PSP[i // 2].ap().bitcast(BF16)
        return b[:, 1024 * (i % 2):1024 * (i % 2) + 1024]

    def kpn(handle, k, n, base=0):
        return bass.AP(handle, base, [[n, 128], [128 * n, k], [1, n]])

    def A(tensor, off, dims):
        return bass.AP(tensor, off, dims)

    dma("sync", masks[:, :], kpn(masks_d, 5, 640), sem="d0")
    for l in range(2):
        dma("sync", qs_rep[:, 64 * l:64 * l + 64], qs_d.ap()[l:l + 1, :].partition_broadcast(128), sem="d0")
        dma("sync", ks_rep[:, 64 * l:64 * l + 64], ks_d.ap()[l:l + 1, :].partition_broadcast(128), sem="d0")
    dma("sync", bsc_rep[:, :], bscale_d.ap().partition_broadcast(128), sem="d0")
    for buf in (ident_b, ident_f):
        mtok = em.op("gpsimd", lambda g, b=buf: g.memset(b[:, :], 0.0), (), inc="g")
        em.op("gpsimd", lambda g, b=buf: g.affine_select(
            out=b[:, :], in_=b[:, :], compare_op=ALU.not_equal, fill=1.0, base=0,
            pattern=[[-1, 128]], channel_multiplier=1), [mtok])
    em.op("gpsimd", lambda g: g.memset(ones_col_b[:, :], 1.0))
    em.op("gpsimd", lambda g: g.memset(ones_row_f[:, :], 1.0))

    em.op("gpsimd", lambda g: g.memset(junk[:, :], 0.0))
    dtok = em.op("gpsimd", lambda g: g.memset(kv_part[0:64, :], 0.0), (), inc="g")
    dtok = em.op("gpsimd", lambda g: g.memset(ksum_part[0:64, :], 0.0), [dtok], inc="g")
    if debug:
        dtok = em.op("gpsimd", lambda g: g.memset(RES[:, :], 0.0), [dtok], inc="g")
    em.op("gpsimd", lambda g: g.memset(A(RES, 0, [[8 * RC, 128], [RC, 8], [1, 1]]), 0.0), [dtok])
    dma("gpsimd", wqkvo[:, :], kpn(wqkvo_h, 8, 3072), sem="wd0")
    em.run_block("p0")
    if n_phases < 1:
        return finish()

    with contextlib.ExitStack() as ph:
        xt = [ph.enter_context(nc.sbuf_tensor(f"xt{i}", [128, 1024], F32)) for i in range(4)]
        xnt = [ph.enter_context(nc.sbuf_tensor(f"xnt{i}", [128, 1024], F32)) for i in range(2)]
        sqs2 = [ph.enter_context(nc.sbuf_tensor(f"sq_p1{i}", [128, 1024], F32)) for i in range(2)]
        st2b = [ph.enter_context(nc.sbuf_tensor(f"st_p1{i}", [128, 8], F32)) for i in range(2)]
        xn_done = [None] * NT
        tr_done = [None] * NT
        var_done = [None] * NT
        evac = {}
        for t in range(NT):
            sqs = sqs2[t % 2]
            st = st2b[t % 2]
            xc = xt[t % 4]
            ld = dma("sync" if t % 2 == 0 else "gpsimd", xc[:, :],
                     x_sl.ap()[128 * t:128 * t + 128, :],
                     [xn_done[t - 4]] if t >= 4 else [],
                     sem=f"d{t % 4}" if t % 2 == 0 else f"gx{(t // 2) % 2}")
            sqt = s_act(sqs[:, :], xc[:, :], AF.Square, waits=[ld], inc="s")
            ssq = v_red(ALU.add, st[:, 2:3], sqs[:, :], waits=[sqt], inc="v")
            v_red(ALU.add, st[:, 0:1], xc[:, :], waits=[ld])
            v_ts(ALU.mult, st[:, 1:2], st[:, 0:1], 1.0 / 1024.0)
            v_ts(ALU.mult, st[:, 3:4], st[:, 2:3], 1.0 / 1024.0, waits=[ssq])
            v_tt(ALU.mult, st[:, 4:5], st[:, 1:2], st[:, 1:2])
            v_tt(ALU.subtract, st[:, 5:6], st[:, 3:4], st[:, 4:5])
            var = v_ts(ALU.add, st[:, 5:6], st[:, 5:6], 1e-5, inc="v")
            var_done[t] = var
            lnv = s_act(st[:, 6:7], st[:, 5:6], AF.Ln, waits=[var], inc="s")
            halfm = v_ts(ALU.mult, st[:, 6:7], st[:, 6:7], -0.5, waits=[lnv], inc="v")
            rstd = s_act(st[:, 6:7], st[:, 6:7], AF.Exp, waits=[halfm], inc="s")
            v_tt(ALU.mult, st[:, 7:8], st[:, 1:2], st[:, 6:7], waits=[rstd])
            nmr = v_ts(ALU.mult, st[:, 7:8], st[:, 7:8], -1.0, inc="v")
            xn_done[t] = s_act(xnt[t % 2][:, :], xc[:, :], AF.Identity,
                               bias=st[:, 7:8], scale=st[:, 6:7],
                               waits=[nmr] + ([tr_done[t - 2]] if t >= 2 else []), inc="s")
            if debug:
                xn_done[t] = dma("sync", dbg2.ap()[128 * t:128 * t + 128, :], xnt[t % 2][:, :],
                                 [xn_done[t]], sem=f"od{t % 2}")

            tr = None
            for k in range(8):
                w = [xn_done[t]] + ([evac.get(7)] if k == 0 and 7 in evac else [])
                tr = t_tr(PS[6 + k % 2][:, 128 * (k // 2):128 * (k // 2) + 128],
                          xnt[t % 2][:, 128 * k:128 * k + 128], ident_f[:, :],
                          w if k == 0 else (), inc="t" if k == 7 else None)
            tr_done[t] = tr
            s_act(A(RES, 1 + 128 * t, [[8 * RC, 128], [2 * RC, 4], [1, 128]]),
                  PS[6][:, 0:512], AF.Copy, waits=[tr])
            evac[7] = s_act(A(RES, RC + 1 + 128 * t, [[8 * RC, 128], [2 * RC, 4], [1, 128]]),
                            PS[7][:, 0:512], AF.Copy, inc="s")
        em.run_block("p1")
    if n_phases < 2:
        return finish()

    with contextlib.ExitStack() as ph:
        xq = [ph.enter_context(nc.sbuf_tensor(f"xq{i}", [128, 1024], BF16)) for i in range(2)]
        qk_f = ph.enter_context(nc.sbuf_tensor("qk_f2", [128, 2048], F32))
        sqv = ph.enter_context(nc.sbuf_tensor("sqv2", [128, 2048], BF16))
        ssq = ph.enter_context(nc.sbuf_tensor("ssq2", [128, 32], F32))
        qsp = ph.enter_context(nc.sbuf_tensor("qsp2", [128, 1024], BF16))
        ksp = ph.enter_context(nc.sbuf_tensor("ksp2", [128, 1024], BF16))
        vtb = ph.enter_context(nc.sbuf_tensor("vtb2", [128, 1024], BF16))
        qk_evac = v_evac = last_sp = last_vmm = last_qdma = None
        kv_mm = None
        kv_acc_tok = None
        for i in range(16):
            xq_tok = v_copy(xq[i % 2][:, :],
                            A(RES, 1 + HALO + 128 * i, [[8 * RC, 128], [RC, 8], [1, 128]]),
                            [last_vmm] if i >= 2 else [], inc="v")
            qk_mm = None
            for nb in range(4):
                w = [xq_tok]
                if i >= 1:
                    w.append(v_evac if nb < 2 else qk_evac)
                for k in range(8):
                    qk_mm = t_mm(PS[nb][:, :], xq[i % 2][:, 128 * k:128 * k + 128],
                                 wqkvo[:, 3072 * k + 512 * nb: 3072 * k + 512 * nb + 512],
                                 k == 0, k == 7, w if k == 0 else (),
                                 inc="t" if (nb == 3 and k == 7) else None)
            s_act(qk_f[:, 0:512], PS[0][:, :], AF.Copy, waits=[qk_mm])
            s_act(qk_f[:, 512:1024], PS[1][:, :], AF.Copy)
            s_act(qk_f[:, 1024:1536], PS[2][:, :], AF.Copy)
            qk_evac = s_act(qk_f[:, 1536:2048], PS[3][:, :], AF.Copy, inc="s")
            v_mm = None
            for nb in range(2):
                for k in range(8):
                    v_mm = t_mm(PS[nb][:, :], xq[i % 2][:, 128 * k:128 * k + 128],
                                wqkvo[:, 3072 * k + 2048 + 512 * nb: 3072 * k + 2048 + 512 * nb + 512],
                                k == 0, k == 7, [qk_evac] if (k == 0 and nb == 0) else (),
                                inc="t" if (nb == 1 and k == 7) else None)
            last_vmm = v_mm
            s_act(vtb[:, 0:512], PS[0][:, :], AF.Copy,
                  waits=[v_mm] + ([kv_mm] if kv_mm else []))
            v_evac = s_act(vtb[:, 512:1024], PS[1][:, :], AF.Copy, inc="s")
            v_tt(ALU.mult, sqv[:, :], qk_f[:, :], qk_f[:, :], waits=[qk_evac])
            v_red(ALU.add, ssq[:, :].unsqueeze(2), A(sqv, 0, [[2048, 128], [64, 32], [1, 64]]))
            sq_tok = v_ts(ALU.max, ssq[:, :], ssq[:, :], 1e-24, inc="v")
            s_act(ssq[:, :], ssq[:, :], AF.Ln, waits=[sq_tok])
            s_gap()
            rsq = s_act(ssq[:, :], ssq[:, :], AF.Exp, scale=-0.5, inc="s")
            sc_tok = v_tt(ALU.mult, A(qk_f, 0, [[2048, 128], [64, 32], [1, 64]]),
                          A(qk_f, 0, [[2048, 128], [64, 32], [1, 64]]),
                          ssq[:, :].unsqueeze(2).broadcast_to([128, 32, 64]),
                          [rsq], inc="v")
            s_act(qk_f[:, 1024:2048], qk_f[:, 1024:2048], AF.Exp, waits=[sc_tok])
            s_act(qk_f[:, 0:1024], qk_f[:, 0:1024], AF.Exp)
            if debug and i == 0:
                dma("sync", dbg4.ap()[:, 0:2048], qk_f[:, :], [sq_tok], sem="od1")
            last_sp = s_act(ksp[:, :], qk_f[:, 1024:2048], AF.Ln, bias=1.0,
                            waits=[kv_mm] if kv_mm else (), inc="s")
            qsp_tok = s_act(qsp[:, :], qk_f[:, 0:1024], AF.Ln, bias=1.0,
                            waits=[last_qdma] if last_qdma else (), inc="s")
            if debug and i == 0:
                dma("sync", dbg4.ap()[:, 2048 + 64:2048 + 64 + 2048], qk_f[:, :], [last_sp], sem="od1")
                dma("sync", dbg4.ap()[:, 2048:2048 + 32], ssq[:, :], [last_sp], sem="od1")
            last_qdma = dma("sync", q_dram.ap()[128 * i:128 * i + 128, :], qsp[:, :], [qsp_tok], sem="od0")
            w0 = [last_sp] + ([kv_acc_tok] if kv_acc_tok else [])
            for h in range(16):
                psd = PS[6] if h < 8 else PS[7]
                t_mm(psd[0:64, 64 * (h % 8):64 * (h % 8) + 64],
                     ksp[:, 64 * h:64 * h + 64], vtb[:, 64 * h:64 * h + 64],
                     True, True, w0 if h == 0 else ())
                kv_mm = t_mm(PS[5][0:64, h:h + 1], ksp[:, 64 * h:64 * h + 64], ones_col_b[:, :],
                             True, True, (), inc="t" if h == 15 else None)
            v_tt(ALU.add, kv_part[0:64, 0:512], kv_part[0:64, 0:512], PS[6][0:64, :], [kv_mm])
            v_tt(ALU.add, kv_part[0:64, 512:1024], kv_part[0:64, 512:1024], PS[7][0:64, :])
            kv_acc_tok = v_tt(ALU.add, ksum_part[0:64, :], ksum_part[0:64, :],
                              PS[5][0:64, 0:16], inc="v")
        v_ts(ALU.mult, kv_part[0:64, 0:512], kv_part[0:64, 0:512], 0.125, waits=[kv_acc_tok])
        v_ts(ALU.mult, kv_part[0:64, 512:1024], kv_part[0:64, 512:1024], 0.125, inc="v")
        em.run_block("p2")
    es_o.close()
    if n_phases < 3:
        return finish()

    def layer_qkv_attn(l):
        with contextlib.ExitStack() as ph:
            INb = [ph.enter_context(nc.sbuf_tensor(f"INb{i}_{l}", [128, 8 * 129], BF16)) for i in range(2)]
            tab = [ph.enter_context(nc.sbuf_tensor(f"tab{i}_{l}", [128, 256], F32)) for i in range(2)]
            qk_f = ph.enter_context(nc.sbuf_tensor(f"qkf_{l}", [128, 1024], F32))
            t2 = ph.enter_context(nc.sbuf_tensor(f"t2_{l}", [128, 1024], F32))
            p1b = ph.enter_context(nc.sbuf_tensor(f"p1b_{l}", [128, 1024], F32))
            ub = ph.enter_context(nc.sbuf_tensor(f"ub_{l}", [128, 1024], F32))
            qkrot = ph.enter_context(nc.sbuf_tensor(f"qkrot_{l}", [128, 1024], BF16))
            sqv = ph.enter_context(nc.sbuf_tensor(f"sqv_{l}", [128, 1024], BF16))
            ssq = ph.enter_context(nc.sbuf_tensor(f"ssq_{l}", [128, 16], F32))
            QFP = ph.enter_context(nc.sbuf_tensor(f"QFP_{l}", [128, 4 * 128], BF16))
            KFP = ph.enter_context(nc.sbuf_tensor(f"KFP_{l}", [128, 4 * 2 * RING * 128], BF16))
            VT = ph.enter_context(nc.sbuf_tensor(f"VT_{l}", [128, RING * 520], BF16))
            PTb = [ph.enter_context(nc.sbuf_tensor(f"PT{i}_{l}", [128, 640], BF16)) for i in range(3)]
            PTT = [ph.enter_context(nc.sbuf_tensor(f"PTT{i}_{l}", [128, 640], BF16)) for i in range(2)]
            negm8 = ph.enter_context(nc.sbuf_tensor(f"negm8_{l}", [128, 8], F32))
            nm1 = ph.enter_context(nc.sbuf_tensor(f"nm1_{l}", [128, 1], F32))
            o_tm = ph.enter_context(nc.sbuf_tensor(f"otm_{l}", [128, 512], BF16))
            rcp = ph.enter_context(nc.sbuf_tensor(f"rcp_{l}", [128, 1], F32))
            ATT = ph.enter_context(nc.sbuf_tensor(f"ATT_{l}", [128, 4 * 2176], BF16))
            wqkv = ph.enter_context(nc.sbuf_tensor(f"wqkv_{l}", [128, 8 * 1536], BF16))
            wo_sb = ph.enter_context(nc.sbuf_tensor(f"wo_{l}", [128, 4 * 1024], BF16))

            wq_tok = dma("gpsimd", wqkv[:, :], kpn(wqkv_h, 8, 1536, base=l * 8 * 128 * 1536), sem="wd0")
            wo_tok = dma("gpsimd", wo_sb[:, :], kpn(wo_h, 4, 1024, base=l * 4 * 128 * 1024), sem="wd1")

            def mslot(t):
                return t - 4 if 5 <= t <= 8 else 0

            wo_sched = {8: [(0, 512, 7)], 12: [(512, 512, 11)], 16: [(1024, 512, 15)],
                        20: [(1536, 512, 19), (2048, 128, 20)]}

            prod_tok = [None] * NT
            attn_tok = [None] * NT
            last_exp = None
            prev_exps = [None, None]
            trPT = []
            pv_last = [None, None]
            stt2 = [None, None]
            prev_stt = None
            last_ptt_ev = None
            prev_evK = None
            last_wo_evac = None
            woev = {0: None, 1: None}

            rot_toks = [None] * NT
            for t in range(NT):
                tab_tok = dma("sync", tab[t % 2][:, :], tabs_d.ap()[128 * t:128 * t + 128, :],
                              [rot_toks[t - 2]] if t >= 2 else (), sem=f"d{t % 2}")
                w = ([prod_tok[t - 2]] if t >= 2 else []) + ([last_wo_evac] if last_wo_evac else [])
                inb = em.op("gpsimd", lambda g, t_=t: g.tensor_copy(
                    INb[t_ % 2][:, :], A(RES, 128 * t_, [[8 * RC, 128], [RC, 8], [1, 129]])),
                    w, inc="g")
                w = [inb, wq_tok]
                if t >= 1:
                    w.append(prod_tok[t - 1])
                if t >= 5 and attn_tok[t - 1] is not None:
                    w.append(attn_tok[t - 1])
                w.append(woev[0])
                w.append(woev[1])
                qk_done = None
                for nb in range(2):
                    for k in range(8):
                        off = 1 if k < 4 else 0
                        qk_done = t_mm(PS[nb][:, :], INb[t % 2][:, 129 * k + off: 129 * k + off + 128],
                                       wqkv[:, 1536 * k + 512 * nb: 1536 * k + 512 * nb + 512],
                                       k == 0, k == 7, w if (nb == 0 and k == 0) else (),
                                       inc="t" if (nb == 1 and k == 7) else None)
                qk0ev = s_act(qk_f[:, 0:512], PS[0][:, :], AF.Copy,
                              waits=[qk_done] + ([rot_toks[t - 1]] if t >= 1 else []), inc="s")
                qke = s_act(qk_f[:, 512:1024], PS[1][:, :], AF.Copy, inc="s")
                qkv_mm = None
                for k in range(8):
                    off = 1 if k < 4 else 0
                    qkv_mm = t_mm(PS[0][:, :], INb[t % 2][:, 129 * k + off: 129 * k + off + 128],
                                  wqkv[:, 1536 * k + 1024: 1536 * k + 1536],
                                  k == 0, k == 7, [qk0ev] if k == 0 else (),
                                  inc="t" if k == 7 else None)
                slot = t % RING
                ones_tok = v_memset(A(VT, 520 * slot + 64, [[RING * 520, 128], [65, 8], [1, 1]]), 1.0, inc="v")
                vev = s_act(A(VT, 520 * slot, [[RING * 520, 128], [65, 8], [1, 64]]),
                            A(PSP[0], 0, [[1024, 128], [64, 8], [1, 64]]), AF.Copy,
                            waits=[qkv_mm], inc="s")
                v_tt(ALU.mult, sqv[:, :], qk_f[:, :], qk_f[:, :], waits=[qke])
                v_red(ALU.add, ssq[:, :].unsqueeze(2), A(sqv, 0, [[1024, 128], [64, 16], [1, 64]]))
                sq_tok = v_ts(ALU.max, ssq[:, :], ssq[:, :], 1e-24, inc="v")
                s_act(ssq[:, :], ssq[:, :], AF.Ln, waits=[sq_tok])
                s_gap()
                rsq_tok = s_act(ssq[:, :], ssq[:, :], AF.Exp, scale=-0.5, inc="s")
                v_tt(ALU.mult, A(t2, 0, [[1024, 128], [64, 16], [1, 64]]),
                     A(qk_f, 0, [[1024, 128], [64, 16], [1, 64]]),
                     ssq[:, 0:16].unsqueeze(2).broadcast_to([128, 16, 64]), [rsq_tok])
                v_tt(ALU.mult, A(t2, 0, [[1024, 128], [64, 8], [1, 64]]),
                     A(t2, 0, [[1024, 128], [64, 8], [1, 64]]),
                     qs_rep[:, None, 64 * l:64 * l + 64].broadcast_to([128, 8, 64]))
                v_tt(ALU.mult, A(t2, 512, [[1024, 128], [64, 8], [1, 64]]),
                     A(t2, 512, [[1024, 128], [64, 8], [1, 64]]),
                     ks_rep[:, None, 64 * l:64 * l + 64].broadcast_to([128, 8, 64]))
                tb = tab[t % 2]
                v_tt(ALU.mult, A(p1b, 0, [[1024, 128], [64, 8], [1, 64]]),
                     A(t2, 0, [[1024, 128], [64, 8], [1, 64]]),
                     tb[:, None, 0:64].broadcast_to([128, 8, 64]), waits=[tab_tok])
                v_tt(ALU.mult, A(p1b, 512, [[1024, 128], [64, 8], [1, 64]]),
                     A(t2, 512, [[1024, 128], [64, 8], [1, 64]]),
                     tb[:, None, 128:192].broadcast_to([128, 8, 64]))
                for (qoff, toff) in ((0, 64), (512, 192)):
                    v_tt(ALU.mult, A(ub, qoff, [[1024, 128], [64, 8], [1, 32]]),
                         A(t2, qoff + 32, [[1024, 128], [64, 8], [1, 32]]),
                         tb[:, None, toff:toff + 32].broadcast_to([128, 8, 32]))
                    v_tt(ALU.mult, A(ub, qoff + 32, [[1024, 128], [64, 8], [1, 32]]),
                         A(t2, qoff, [[1024, 128], [64, 8], [1, 32]]),
                         tb[:, None, toff + 32:toff + 64].broadcast_to([128, 8, 32]))
                rot_tok = v_tt(ALU.add, qkrot[:, :], p1b[:, :], ub[:, :], inc="v")
                rot_toks[t] = rot_tok
                for blkn in range(8):
                    w = []
                    if blkn == 0:
                        w = [rot_tok] + ([attn_tok[t - 1]] if t >= 5 and attn_tok[t - 1] else []) \
                            + ([prev_evK] if prev_evK else [])
                    tr = t_tr(ps_bf(6 + blkn // 4)[:, 128 * (blkn % 4):128 * (blkn % 4) + 128],
                              qkrot[:, 128 * blkn:128 * blkn + 128], ident_b[:, :], w,
                              inc="t" if blkn == 7 else None)
                s_act(QFP[:, 0:512], ps_bf(6)[:, 0:512], AF.Copy, waits=[tr])
                prev_evK = evK = s_act(A(KFP, 128 * slot, [[6144, 128], [1536, 4], [1, 128]]),
                                       ps_bf(7)[:, 0:512], AF.Copy, inc="s")
                dupK = em.op("gpsimd", lambda g, sl_=slot: g.tensor_copy(
                    A(KFP, 128 * (sl_ + RING), [[6144, 128], [1536, 4], [1, 128]]),
                    A(KFP, 128 * sl_, [[6144, 128], [1536, 4], [1, 128]])), [evK], inc="g")
                prod_tok[t] = em.op("scalar", lambda e: e.copy(junk[:, 0:2], junk[:, 0:2]),
                                    [dupK, vev, ones_tok], inc="s")

                if t >= 4:
                    ms = mslot(t)
                    s0 = (t - 4) % RING
                    exps = [None] * 8
                    exps1 = [None] * 8

                    def emit_scores(h):
                        p, pb = h // 2, 64 * (h % 2)
                        Be = PSP[1] if h % 2 == 0 else PSP[2]
                        if h < 2:
                            w0 = [prod_tok[t]]
                            if prev_exps[h % 2] is not None:
                                w0.append(prev_exps[h % 2])
                        else:
                            w0 = [exps[h - 2]]
                        t_mm(Be[:, 0:512], QFP[pb:pb + 64, 128 * p:128 * p + 128],
                             KFP[pb:pb + 64, 1536 * p + 128 * s0:1536 * p + 128 * s0 + 512],
                             True, False, w0)
                        t_mm(Be[:, 0:512], ident_b[:, :], masks[:, 640 * ms:640 * ms + 512],
                             False, True)
                        t_mm(Be[:, 512:640], QFP[pb:pb + 64, 128 * p:128 * p + 128],
                             KFP[pb:pb + 64, 1536 * p + 128 * s0 + 512:1536 * p + 128 * s0 + 640],
                             True, False)
                        sc_mm = t_mm(Be[:, 512:640], ident_b[:, :],
                                     masks[:, 640 * ms + 512:640 * ms + 640],
                                     False, True, (), inc="t")
                        mn = v_red(ALU.max, negm8[:, h:h + 1], Be[:, 0:640], negate=True,
                                   waits=[sc_mm], inc="v")
                        we = [mn]
                        g = len(trPT)
                        if g >= 3 and trPT[g - 3] is not None:
                            we.append(trPT[g - 3])
                        trPT.append(None)
                        exps[h] = s_act(PTb[g % 3][:, 0:640], Be[:, 0:640], AF.Exp,
                                        bias=negm8[:, h:h + 1], waits=we, inc="s")
                        return g

                    def emit_trs(h, g):
                        nonlocal last_ptt_ev
                        hp = h % 2
                        tr = None
                        for c in range(5):
                            wt = []
                            if c == 0:
                                wt = [exps[h], woev[0], woev[1]]
                                if last_ptt_ev is not None:
                                    wt.append(last_ptt_ev)
                            tr = t_tr(ps_bf(c % 2)[:, 128 * (c // 2):128 * (c // 2) + 128],
                                      PTb[g % 3][:, 128 * c:128 * c + 128],
                                      ident_b[:, :], wt, inc="t" if c == 4 else None)
                        trPT[g] = tr
                        wev = [tr]
                        if pv_last[hp] is not None:
                            wev.append(pv_last[hp])
                        s_act(A(PTT[hp], 0, [[640, 128], [256, 3], [1, 128]]),
                              ps_bf(0)[:, 0:384], AF.Copy, waits=wev)
                        last_ptt_ev = s_act(A(PTT[hp], 128, [[640, 128], [256, 2], [1, 128]]),
                                            ps_bf(1)[:, 0:256], AF.Copy, inc="s")
                        ptt_ev[h] = last_ptt_ev

                    def emit_pv(h):
                        nonlocal prev_stt
                        hp = h % 2
                        Bp = PS[7] if hp == 0 else PS[6]
                        pv = None
                        for c in range(5):
                            sl = (t - 4 + c) % RING
                            wt = ()
                            if c == 0:
                                wt = [ptt_ev[h]] + ([stt2[hp]] if stt2[hp] else [])
                            pv = t_mm(Bp[:, 0:65], PTT[hp][:, 128 * c:128 * c + 128],
                                      VT[:, 520 * sl + 65 * h: 520 * sl + 65 * h + 65],
                                      c == 0, c == 4, wt, inc="t" if c == 4 else None)
                        pv_last[hp] = pv
                        v_recip(rcp[:, :], Bp[:, 64:65], waits=[pv])
                        stt2[hp] = prev_stt = v_ts(ALU.mult, o_tm[:, 64 * h:64 * h + 64],
                                                   Bp[:, 0:64], rcp[:, :], inc="v")

                    gs = [None] * 8
                    ptt_ev = [None] * 8
                    for h in range(8):
                        gs[h] = emit_scores(h)
                        if h >= 1:
                            emit_trs(h - 1, gs[h - 1])
                        if h >= 2:
                            emit_pv(h - 2)
                    emit_trs(7, gs[7])
                    emit_pv(6)
                    emit_pv(7)
                    prev_exps = [exps[6], exps[7]]
                    last_exp = exps[7]
                    atr = None
                    for blkn in range(4):
                        w = [prev_stt, last_ptt_ev, woev[0], woev[1]] if blkn == 0 else []
                        atr = t_tr(ps_bf(blkn % 2)[:, 128 * (blkn // 2):128 * (blkn // 2) + 128],
                                   o_tm[:, 128 * blkn:128 * blkn + 128], ident_b[:, :], w,
                                   inc="t" if blkn == 3 else None)
                    s_act(A(ATT, 128 * (t - 4), [[4 * 2176, 128], [2 * 2176, 2], [1, 128]]),
                          ps_bf(0)[:, 0:256], AF.Copy, waits=[atr])
                    attn_tok[t] = s_act(A(ATT, 2176 + 128 * (t - 4), [[4 * 2176, 128], [2 * 2176, 2], [1, 128]]),
                                        ps_bf(1)[:, 0:256], AF.Copy, inc="s")

                for (c0, bl, tneed) in wo_sched.get(t, ()):
                    for oc in range(8):
                        par = oc % 2
                        w = [attn_tok[tneed], wo_tok, prod_tok[t]]
                        if attn_tok[t] is not None:
                            w.append(attn_tok[t])
                        if woev[par] is not None:
                            w.append(woev[par])
                        wo_mm = None
                        for kc in range(4):
                            wo_mm = t_mm(PS[par][:, 0:bl],
                                         wo_sb[:, 1024 * kc + 128 * oc: 1024 * kc + 128 * oc + 128],
                                         ATT[:, 2176 * kc + c0: 2176 * kc + c0 + bl],
                                         kc == 0, kc == 3, w if kc == 0 else (),
                                         inc="t" if kc == 3 else None)
                        woev[par] = v_tt(ALU.add, res_ap(oc, 512 + c0, bl),
                                         res_ap(oc, 512 + c0, bl), PS[par][:, 0:bl],
                                         [wo_mm], inc="v")
                        last_wo_evac = woev[par]
            if debug:
                for cc in range(4):
                    dma("gpsimd", dbg4.ap()[:, 512 * cc:512 * cc + 512],
                        ATT[:, 2176 * cc + 1280: 2176 * cc + 1792], [attn_tok[20]], sem="gd")
            em.run_block(f"attn_{l}")

    def layer_mlp(l, half, tail_oc=None):
        i0 = HALO + 1024 * half
        with contextlib.ExitStack() as ph:
            hs = ph.enter_context(nc.sbuf_tensor(f"hs_{l}_{half}", [128, 8 * 512], BF16))
            LN1 = ph.enter_context(nc.sbuf_tensor(f"LN1_{l}_{half}", [128, 8 * 1024], BF16))
            MH = ph.enter_context(nc.sbuf_tensor(f"MH_{l}_{half}", [128, NPAIR * 1024], BF16))
            w1c = [ph.enter_context(nc.sbuf_tensor(f"w1c{i}_{l}_{half}", [128, 8 * 256], BF16)) for i in range(3)]
            w2c = [ph.enter_context(nc.sbuf_tensor(f"w2c{i}_{l}_{half}", [128, NPAIR * 128], BF16)) for i in range(2)]
            gl = ph.enter_context(nc.sbuf_tensor(f"gl_{l}_{half}", [128, 512], BF16))
            tmp = ph.enter_context(nc.sbuf_tensor(f"tmp_{l}_{half}", [128, 512], F32))
            sp = ph.enter_context(nc.sbuf_tensor(f"sp_{l}_{half}", [1, 3 * 512], F32))

            dma("sync", b1_sb[:, :], b1_h.ap()[l], sem="d3")
            dma("sync", b2_sb[:, :], b2_h.ap()[l], sem="d3")
            dma("sync", lng_sb[:, :], lng_h.ap()[l], sem="d3")
            btok = dma("sync", lnb_sb[:, :], lnb_h.ap()[l], sem="d3")

            w1tok = [None] * NPAIR
            for i in range(3):
                w1tok[i] = dma("gpsimd", w1c[i][:, :],
                               kpn(w1_h, 8, 256, base=(l * NPAIR + i) * 8 * 128 * 256), sem=f"wd{i % 3}")

            ln_tok = [None, None]
            prev_stats = None
            prev_lt = None
            for bk in range(2):
                j0 = i0 + 512 * bk
                v_copy(A(hs, 0, [[4096, 128], [512, 4], [1, 512]]),
                       A(RES, 1 + j0, [[8 * RC, 128], [RC, 4], [1, 512]]),
                       [prev_stats] if prev_stats else ())
                cast2 = v_copy(A(hs, 2048, [[4096, 128], [512, 4], [1, 512]]),
                               A(RES, 4 * RC + j0, [[8 * RC, 128], [RC, 4], [1, 512]]), inc="v")
                sum_mm = None
                for k in range(8):
                    sum_mm = t_mm(PS[6][0:1, 0:512], ones_col_b[:, :], hs[:, 512 * k:512 * k + 512],
                                  k == 0, k == 7,
                                  ([cast2] + ([prev_lt] if prev_lt else [])) if k == 0 else (),
                                  inc="t" if k == 7 else None)
                sq_tok = s_act(hs[:, :], hs[:, :], AF.Square, waits=[sum_mm], inc="s")
                st_mm = None
                for k in range(8):
                    st_mm = t_mm(PS[7][0:1, 0:512], ones_col_b[:, :], hs[:, 512 * k:512 * k + 512],
                                 k == 0, k == 7, [sq_tok] if k == 0 else (),
                                 inc="t" if k == 7 else None)
                prev_stats = st_mm
                v_ts(ALU.mult, sp[0:1, 0:512], PS[6][0:1, 0:512], -1.0 / 1024.0, waits=[st_mm])
                v_ts(ALU.mult, sp[0:1, 512:1024], PS[7][0:1, 0:512], 1.0 / 1024.0)
                v_tt(ALU.mult, sp[0:1, 1024:1536], sp[0:1, 0:512], sp[0:1, 0:512])
                v_tt(ALU.subtract, sp[0:1, 512:1024], sp[0:1, 512:1024],
                     sp[0:1, 1024:1536])
                var_tok = v_ts(ALU.add, sp[0:1, 512:1024], sp[0:1, 512:1024], 1e-5, inc="v")
                lnv = s_act(sp[0:1, 512:1024], sp[0:1, 512:1024], AF.Ln, waits=[var_tok], inc="s")
                halfm = v_ts(ALU.mult, sp[0:1, 512:1024], sp[0:1, 512:1024], -0.5, waits=[lnv], inc="v")
                rstd_tok = s_act(sp[0:1, 512:1024], sp[0:1, 512:1024], AF.Exp, waits=[halfm], inc="s")
                nmr_tok = v_tt(ALU.mult, sp[0:1, 1024:1536], sp[0:1, 0:512], sp[0:1, 512:1024],
                               [rstd_tok], inc="v")
                br1 = t_mm(PS[6][:, :], ones_row_f[:, :], sp[0:1, 512:1024], True, True,
                           [rstd_tok], inc="t")
                br2 = t_mm(PS[7][:, :], ones_row_f[:, :], sp[0:1, 1024:1536], True, True,
                           [nmr_tok], inc="t")
                lt = None
                for k in range(8):
                    sh_ = 1 if k >= 4 else 0
                    t1 = v_tt(ALU.mult, tmp[:, :],
                              A(RES, RC * k + 1 + j0 - sh_, [[8 * RC, 128], [1, 512]]),
                              PS[6][:, :], [br1, br2] + ([lt] if lt else []))
                    t2_ = v_tt(ALU.add, tmp[:, :], tmp[:, :], PS[7][:, :], inc="v")
                    lt = v_stt(LN1[:, 1024 * k + 512 * bk: 1024 * k + 512 * bk + 512], tmp[:, :],
                               lng_sb[:, k:k + 1], lnb_sb[:, k:k + 1].broadcast_to([128, 512]),
                               ALU.mult, ALU.add, [t2_, btok], inc="v")
                ln_tok[bk] = lt
                prev_lt = lt

            mh_tok = None
            g_mm_last = None
            for i in range(NPAIR):
                for bk in range(2):
                    par = 2 * (i % 2)
                    w = [w1tok[i], ln_tok[bk]]
                    if (i >= 2 and bk == 0) or bk == 1:
                        w.append(mh_tok)
                    a_mm = g_mm = None
                    for k in range(8):
                        a_mm = t_mm(PS[par][:, :], w1c[i % 3][:, 256 * k: 256 * k + 128],
                                    LN1[:, 1024 * k + 512 * bk: 1024 * k + 512 * bk + 512],
                                    k == 0, k == 7, w if k == 0 else ())
                        g_mm = t_mm(PS[par + 1][:, :], w1c[i % 3][:, 256 * k + 128: 256 * k + 256],
                                    LN1[:, 1024 * k + 512 * bk: 1024 * k + 512 * bk + 512],
                                    k == 0, k == 7, (), inc="t" if k == 7 else None)
                    gelu_tok = s_act(gl[:, :], PS[par + 1][:, :], AF.Gelu,
                                     bias=b1_sb[:, 2 * i + 1:2 * i + 2],
                                     waits=[g_mm, btok] + ([mh_tok] if mh_tok else []), inc="s")
                    mh_tok = v_stt(MH[:, 1024 * i + 512 * bk: 1024 * i + 512 * bk + 512],
                                   PS[par][:, :], b1_sb[:, 2 * i:2 * i + 1], gl[:, :],
                                   ALU.add, ALU.mult, [gelu_tok], inc="v")
                g_mm_last = g_mm
                if i + 3 < NPAIR:
                    w1tok[i + 3] = dma("gpsimd", w1c[(i + 3) % 3][:, :],
                                       kpn(w1_h, 8, 256, base=(l * NPAIR + i + 3) * 8 * 128 * 256),
                                       [mh_tok], sem=f"wd{(i + 3) % 3}")

            w2tok = [None] * 8
            w2tok[0] = dma("gpsimd", w2c[0][:, :], kpn(w2_h, NPAIR, 128, base=(l * 8 + 0) * NPAIR * 128 * 128), sem="wd0")
            w2tok[1] = dma("gpsimd", w2c[1][:, :], kpn(w2_h, NPAIR, 128, base=(l * 8 + 1) * NPAIR * 128 * 128), sem="wd1")
            evtok = {4: None, 5: None}
            for oc in range(8):
                for bk in range(2):
                    par = 4 + (2 * oc + bk) % 2
                    w = [w2tok[oc], mh_tok]
                    if evtok[par]:
                        w.append(evtok[par])
                    w2_mm = None
                    for k in range(NPAIR):
                        w2_mm = t_mm(PS[par][:, :], w2c[oc % 2][:, 128 * k:128 * k + 128],
                                     MH[:, 1024 * k + 512 * bk: 1024 * k + 512 * bk + 512],
                                     k == 0, k == NPAIR - 1, w if k == 0 else (),
                                     inc="t" if k == NPAIR - 1 else None)
                    evtok[par] = v_stt(res_ap(oc, i0 + 512 * bk, 512), PS[par][:, :],
                                       b2_sb[:, oc:oc + 1], res_ap(oc, i0 + 512 * bk, 512),
                                       ALU.add, ALU.add, [w2_mm, btok], inc="v")
                if tail_oc is not None:
                    tail_oc(oc, evtok[4], evtok[5])
                if oc + 2 < 8:
                    w2tok[oc + 2] = dma("gpsimd", w2c[oc % 2][:, :],
                                        kpn(w2_h, NPAIR, 128, base=(l * 8 + oc + 2) * NPAIR * 128 * 128),
                                        [evtok[4 + (2 * oc + 1) % 2]], sem=f"wd{oc % 2}")
            em.run_block(f"mlp_{l}_{half}")

    def cc_stage_head():
        dma("sync", bass.AP(cc_in, 0, [[1024, 64], [1, 1024]]), kv_part[0:64, :], sem="d0")
        dma("sync", bass.AP(cc_in, 512 * 128, [[16, 64], [1, 16]]), ksum_part[0:64, :], sem="d0")

    def cc_stage_oc(oc, e4, e5):
        return dma("sync", bass.AP(cc_in, (520 + 640 * oc) * 128, [[640, 128], [1, 640]]),
                   res_ap(oc, TOWN, HALO), [e4, e5], sem="d0")

    def cc_launch(stage_tok):
        em.op("gpsimd", lambda g: g.collective_compute(
            "AllGather", ALU.bypass, replica_groups=[list(range(NCORES))],
            ins=[cc_in.ap().opt()], outs=[cc_out.ap().opt()]), [stage_tok], inc="cc")
        return ("cc", em.counts["cc"])

    def cc_recv(cctok):
        with contextlib.ExitStack() as ph:
            kv8 = ph.enter_context(nc.sbuf_tensor("kv8", [64, 8 * 1024], F32))
            ks8 = ph.enter_context(nc.sbuf_tensor("ks8", [64, 8 * 16], F32))
            ldts = []
            for r in range(NCORES):
                q, sm = (("sync", "d0") if r % 2 == 0 else ("scalar", "d2"))
                dma(q, kv8[0:64, 1024 * r:1024 * (r + 1)],
                    bass.AP(cc_out, CCROWS * 128 * r, [[1024, 64], [1, 1024]]), [cctok], sem=sm)
                ldts.append(dma(q, ks8[0:64, 16 * r:16 * (r + 1)],
                                bass.AP(cc_out, CCROWS * 128 * r + 512 * 128, [[16, 64], [1, 16]]),
                                (), sem=sm))
            ldt2 = ldts[-2]
            ldt = ldts[-1]

            res_dsts = [res_ap(k, 0, HALO) for k in range(8)]
            sems = em.sems

            def _halo(e, ks, dsts=tuple(res_dsts)):
                pid = e.partition_id()
                with e.register("off") as off, e.register("tmr") as tmr:
                    e.reg_alu(off, pid, NCORES - 1, ALU.add)
                    e.reg_alu(tmr, off, NCORES, ALU.divide)
                    e.reg_alu(tmr, tmr, NCORES, ALU.mult)
                    e.reg_alu(off, off, tmr, ALU.subtract)
                    e.reg_alu(off, off, CCROWS * 128, ALU.mult)
                    inst = None
                    for k in ks:
                        with e.register(f"offk{k}") as offk:
                            e.reg_alu(offk, off, (520 + 640 * k) * 128, ALU.add)
                            src = bass.AP(cc_out, offk, [[640, 128], [1, 640]])
                            inst = e.dma_start(out=dsts[k], in_=src)
                            inst.then_inc(sems["bd"], 16)
                    return inst

            em.counts["bd"] += 16 * 8
            halo_tok = ("bd", em.counts["bd"])
            em.op("scalar", lambda e: _halo(e, (0, 1, 2, 3)), [cctok])
            em.op("sync", lambda e: _halo(e, (4, 5, 6, 7)), [cctok])
            sc_tok = None
            for k in range(8):
                sc_tok = v_ts(ALU.mult, res_ap(k, 0, HALO), res_ap(k, 0, HALO), bsc_rep[:, :],
                              [halo_tok] if k == 0 else (), inc="v" if k == 7 else None)
            v_tt(ALU.add, kv_part[0:64, :], kv8[0:64, 0:1024], kv8[0:64, 1024:2048], [ldt, ldt2])
            for r in range(2, 8):
                v_tt(ALU.add, kv_part[0:64, :], kv_part[0:64, :], kv8[0:64, 1024 * r:1024 * (r + 1)])
            v_tt(ALU.add, ksum_part[0:64, :], ks8[0:64, 0:16], ks8[0:64, 16:32])
            for r in range(2, 8):
                v_tt(ALU.add, ksum_part[0:64, :], ksum_part[0:64, :], ks8[0:64, 16 * r:16 * (r + 1)])
            v_ts(ALU.add, ksum_part[0:64, :], ksum_part[0:64, :], 1e-6)
            v_copy(A(kvaug, 0, [[16 * 65, 64], [65, 16], [1, 64]]),
                   A(kv_part, 0, [[1024, 64], [64, 16], [1, 64]]))
            kva = v_copy(A(kvaug, 64, [[16 * 65, 64], [65, 16], [1, 1]]),
                         ksum_part[0:64, :].unsqueeze(2), inc="v")
            dup = dma("sync", kvaug[64:128, :], kvaug[0:64, :], [kva], sem="bd")
            em.run_block("cc_recv")
            return dup

    layer_qkv_attn(0)
    if n_phases < 4:
        return finish()
    cc_stage_head()
    _stage = [None]

    def _tail_oc(oc, e4, e5):
        _stage[0] = cc_stage_oc(oc, e4, e5)

    layer_mlp(0, 1, tail_oc=_tail_oc)
    if n_phases < 5:
        return finish()
    cctok = cc_launch(_stage[0])
    layer_mlp(0, 0)
    if n_phases < 6:
        return finish()
    kv_dup = cc_recv(cctok)
    if n_phases < 7:
        return finish()
    layer_qkv_attn(1)
    if n_phases < 8:
        return finish()
    layer_mlp(1, 1)
    layer_mlp(1, 0)
    if n_phases < 9:
        return finish()

    with contextlib.ExitStack() as ph:
        xt = [ph.enter_context(nc.sbuf_tensor(f"fxt{i}", [128, 1024], F32)) for i in range(2)]
        qt = [ph.enter_context(nc.sbuf_tensor(f"fqt{i}", [128, 1024], BF16)) for i in range(2)]
        qfm = ph.enter_context(nc.sbuf_tensor("fqfm", [128, 1024], BF16))
        OUT = [ph.enter_context(nc.sbuf_tensor(f"fout{i}", [128, 1024], F32)) for i in range(2)]
        rcp = ph.enter_context(nc.sbuf_tensor("frcp", [128, 1], F32))
        ost = [None, None]
        prev_ev = None
        prev_qcp = None
        prev_mm = None
        prev_h = None
        for i in range(16):
            j0 = HALO + 128 * i
            w = [ost[i % 2]] if ost[i % 2] else []
            xl = dma("sync", xt[i % 2][:, :], x_sl.ap()[j0:j0 + 128, :], w, sem=f"d{i % 2}")
            ql = dma("scalar", qt[i % 2][:, :], q_dram.ap()[128 * i:128 * i + 128, :], w, sem=f"d{2 + i % 2}")
            tr = None
            for k in range(8):
                wt = []
                if k == 0:
                    wt = [xl] + ([prev_ev] if prev_ev else [])
                tr = t_tr(PS[k % 2][:, 128 * (k // 2):128 * (k // 2) + 128], res_ap(k, j0, 128),
                          ident_f[:, :], wt, inc="t" if k == 7 else None)
            v_tt(ALU.add, A(OUT[i % 2], 0, [[1024, 128], [256, 4], [1, 128]]),
                 PS[0][:, 0:512], A(xt[i % 2], 0, [[1024, 128], [256, 4], [1, 128]]), [tr])
            prev_ev = v_tt(ALU.add, A(OUT[i % 2], 128, [[1024, 128], [256, 4], [1, 128]]),
                           PS[1][:, 0:512], A(xt[i % 2], 128, [[1024, 128], [256, 4], [1, 128]]),
                           inc="v")
            trq = None
            for blkn in range(8):
                wt = []
                if blkn == 0:
                    wt = [ql] + ([prev_qcp] if prev_qcp else [])
                trq = t_tr(ps_bf(2 + blkn % 2)[:, 128 * (blkn // 2):128 * (blkn // 2) + 128],
                           qt[i % 2][:, 128 * blkn:128 * blkn + 128], ident_b[:, :],
                           wt, inc="t" if blkn == 7 else None)
            wq = [trq] + ([prev_mm] if prev_mm else [])
            s_act(A(qfm, 0, [[1024, 128], [256, 4], [1, 128]]),
                  ps_bf(2)[:, 0:512], AF.Copy, waits=wq)
            prev_qcp = s_act(A(qfm, 128, [[1024, 128], [256, 4], [1, 128]]),
                             ps_bf(3)[:, 0:512], AF.Copy, inc="s")
            for h in range(16):
                p, pb = h // 2, 64 * (h % 2)
                par = 4 + h % 2
                prev_mm = t_mm(PS[par][:, 0:65], qfm[pb:pb + 64, 128 * p:128 * p + 128],
                               kvaug[pb:pb + 64, 65 * h:65 * h + 65], True, True,
                               [prev_qcp, kv_dup] + ([prev_h] if prev_h else []), inc="t")
                v_recip(rcp[:, :], PS[par][:, 64:65], waits=[prev_mm])
                prev_h = v_stt(OUT[i % 2][:, 64 * h:64 * h + 64], PS[par][:, 0:64], rcp[:, :],
                               OUT[i % 2][:, 64 * h:64 * h + 64], ALU.mult, ALU.add, inc="v")
            ost[i % 2] = dma("gpsimd", out_d.ap()[128 * i:128 * i + 128, :], OUT[i % 2][:, :],
                             [prev_h], sem=f"go{i % 2}")
        em.run_block("finale")

    return finish()


def kernel(**inputs):
    sh = _prep_shared(inputs)
    in_maps = [_prep_core(inputs, sh, c) for c in range(NCORES)]
    nc = build_program()
    res = run_bass_kernel_spmd(nc, in_maps, core_ids=list(range(NCORES)))
    out = np.concatenate([res.results[c]["out_d"] for c in range(NCORES)], 0)
    return out.reshape(1, N, D)


# version 14
# speedup vs baseline: 1.0148x; 1.0148x over previous
import contextlib
import numpy as np
import ml_dtypes

import concourse.bass as bass
import concourse.mybir as mybir
from concourse.bass_utils import run_bass_kernel_spmd

F32 = mybir.dt.float32
BF16 = mybir.dt.bfloat16
AF = mybir.ActivationFunctionType
ALU = mybir.AluOpType
AX = mybir.AxisListType

NCORES = 8
N = 16384
D = 1024
TOWN = 2048
HALO = 640
TEXT = TOWN + HALO
NT = TEXT // 128
RING = 6
RC = TEXT + 1
HID = 2730
HIDP = 2816
NPAIR = 22
CCROWS = 512 + 8 + 5120
NEG = -1e30

ENGS = ("sync", "gpsimd", "vector", "scalar", "tensor")
ONE_INC = {"t", "v", "s", "g", "cc"}
SERIAL = [False]



def _prep_shared(inp):
    f32 = np.float32
    sh = {}
    sh["wqkvo_h"] = np.ascontiguousarray(inp["L0_attn_W_qkv"].T).reshape(8, 128, 3072)
    sh["wqkv_h"] = np.ascontiguousarray(inp["L0_ffn_wqkv"]).reshape(2, 8, 128, 1536)
    sh["wo_h"] = np.ascontiguousarray(inp["L0_ffn_wo"]).reshape(2, 4, 128, 1024)

    w1 = inp["L0_ffn_w1"]
    w1a = np.zeros((2, 1024, HIDP), f32); w1a[:, :, :HID] = w1[:, :, :HID]
    w1g = np.zeros((2, 1024, HIDP), f32); w1g[:, :, :HID] = w1[:, :, HID:]
    w1_h = np.zeros((2, NPAIR, 8, 128, 256), f32)
    for i in range(NPAIR):
        for k in range(8):
            w1_h[:, i, k, :, 0:128] = w1a[:, 128 * k:128 * (k + 1), 128 * i:128 * (i + 1)]
            w1_h[:, i, k, :, 128:256] = w1g[:, 128 * k:128 * (k + 1), 128 * i:128 * (i + 1)]
    sh["w1_h"] = w1_h

    w2 = inp["L0_ffn_w2"]
    w2p = np.zeros((2, HIDP, 1024), f32); w2p[:, :HID] = w2
    w2_h = np.zeros((2, 8, NPAIR, 128, 128), f32)
    for oc in range(8):
        for k in range(NPAIR):
            w2_h[:, oc, k] = w2p[:, 128 * k:128 * (k + 1), 128 * oc:128 * (oc + 1)]
    sh["w2_h"] = w2_h

    b1 = inp["L0_ffn_b1"]
    b1a = np.zeros((2, HIDP), f32); b1a[:, :HID] = b1[:, :HID]
    b1g = np.zeros((2, HIDP), f32); b1g[:, :HID] = b1[:, HID:]
    b1_h = np.zeros((2, 128, 2 * NPAIR), f32)
    for i in range(NPAIR):
        b1_h[:, :, 2 * i] = b1a[:, 128 * i:128 * (i + 1)]
        b1_h[:, :, 2 * i + 1] = b1g[:, 128 * i:128 * (i + 1)]
    sh["b1_h"] = b1_h
    sh["b2_h"] = np.ascontiguousarray(inp["L0_ffn_b2"].reshape(2, 8, 128).transpose(0, 2, 1))
    sh["lng_h"] = np.ascontiguousarray(inp["L0_ffn_ln_g"].reshape(2, 8, 128).transpose(0, 2, 1))
    sh["lnb_h"] = np.ascontiguousarray(inp["L0_ffn_ln_b"].reshape(2, 8, 128).transpose(0, 2, 1))
    sh["qs_d"] = np.ascontiguousarray(inp["L0_ffn_q_scale"])
    sh["ks_d"] = np.ascontiguousarray(inp["L0_ffn_k_scale"])
    return sh


def _rotary_tabs(t):
    f32 = np.float32
    half_idx = np.arange(0, 64, 2, dtype=f32)
    inv_freq = (1.0 / (f32(10000.0) ** (half_idx / f32(64.0)))).astype(f32)
    t = t.astype(f32)
    freqs = (t[:, None] * inv_freq[None, :]).astype(f32)
    base = ((half_idx + f32(0.4 * 64)) / f32(1.4 * 64)).astype(f32)
    power = ((t - f32(N // 2)) / f32(256.0)).astype(f32)
    scale = (base[None, :] ** power[:, None]).astype(f32)
    cos = np.cos(freqs).astype(f32)
    sin = np.sin(freqs).astype(f32)
    c64 = np.concatenate([cos, cos], 1)
    s64 = np.concatenate([sin, sin], 1)
    sc64 = np.concatenate([scale, scale], 1)
    sgn = np.ones(64, f32); sgn[:32] = -1.0
    tabs = np.empty((len(t), 256), f32)
    tabs[:, 0:64] = 8.0 * c64 * sc64
    tabs[:, 64:128] = sgn * 8.0 * s64 * sc64
    tabs[:, 128:192] = c64 / sc64
    tabs[:, 192:256] = sgn * s64 / sc64
    return tabs


def _masks_for(core):
    i = np.arange(128)[:, None]
    j = np.arange(640)[None, :]
    m0 = np.where((j >= i) & (j <= i + 512), 0.0, NEG).astype(np.float32)
    masks = np.stack([m0] * 5)
    if core == 0:
        for slot in range(1, 5):
            qb = 128 * (slot - 1)
            masks[slot] = np.where(j < 512 - qb, NEG, m0)
    return masks.astype(ml_dtypes.bfloat16)


def _prep_core(inp, sh, core):
    f32 = np.float32
    s = TOWN * core
    x = inp["L0_attn_x"].reshape(N, D)
    lo = s - HALO
    if lo >= 0:
        x_sl = np.ascontiguousarray(x[lo:s + TOWN])
    else:
        x_sl = np.concatenate([np.zeros((-lo, D), f32), x[0:s + TOWN]], 0)
    m = {"x_sl": x_sl,
         "tabs_d": _rotary_tabs(np.arange(lo, s + TOWN)),
         "masks_d": _masks_for(core),
         "bscale_d": np.array([[0.0 if core == 0 else 1.0]], f32)}
    m.update(sh)
    return m



class Emitter:
    def __init__(self, nc, es):
        self.nc = nc
        self.streams = {e: [] for e in ENGS}
        self.sems = {}
        self.counts = {}
        self.dma_sems = ("d0", "d1", "d2", "d3", "wd0", "wd1", "wd2", "bd", "od0", "od1",
                         "gd", "go0", "go1", "gx0", "gx1")
        for name in ("t", "v", "s", "g", "cc") + self.dma_sems:
            self.sems[name] = es.enter_context(nc.semaphore("sem_" + name))
            self.counts[name] = 0
        self._lastw = {}

    def op(self, eng, fn, waits=(), inc=None):
        if SERIAL[0]:
            waits = [(s, v) for s, v in self.counts.items() if v > 0]
        tok = None
        if inc is not None:
            self.counts[inc] += 1 if inc in ONE_INC else 16
            tok = (inc, self.counts[inc])
        waits = tuple((s, v) for t_ in waits if t_ is not None for (s, v) in (t_,) if v > 0)
        sems = self.sems

        def thunk(e, waits=waits, fn=fn, inc=inc, eng=eng, self=self):
            for s, v in waits:
                key = (eng, s)
                if self._lastw.get(key, -1) < v:
                    e.wait_ge(sems[s], v)
                    self._lastw[key] = v
            inst = fn(e)
            if inc is not None:
                inst.then_inc(sems[inc], 1 if inc in ONE_INC else 16)
            return inst

        self.streams[eng].append(thunk)
        return tok

    def run_block(self, name):
        for sname in self.dma_sems:
            v = self.counts[sname]
            if v > 0:
                self.streams["sync"].append(
                    lambda e, s=self.sems[sname], v=v: e.wait_ge(s, v))
        nc = self.nc
        streams = self.streams
        self._lastw = {}
        with nc.Block(name) as blk:
            for eng in ENGS:
                if not streams[eng]:
                    continue

                def make(eng):
                    thunks = streams[eng]

                    def body(e):
                        for th in thunks:
                            th(e)
                    return body
                getattr(blk, eng)(make(eng))
        self.streams = {e: [] for e in ENGS}



def build_program(n_phases=99, debug=False):
    nc = bass.Bass("TRN2", target_bir_lowering=False)

    x_sl = nc.dram_tensor("x_sl", [TEXT, D], F32, kind="ExternalInput")
    tabs_d = nc.dram_tensor("tabs_d", [TEXT, 256], F32, kind="ExternalInput")
    masks_d = nc.dram_tensor("masks_d", [5, 128, 640], BF16, kind="ExternalInput")
    bscale_d = nc.dram_tensor("bscale_d", [1, 1], F32, kind="ExternalInput")
    wqkvo_h = nc.dram_tensor("wqkvo_h", [8, 128, 3072], F32, kind="ExternalInput")
    wqkv_h = nc.dram_tensor("wqkv_h", [2, 8, 128, 1536], F32, kind="ExternalInput")
    wo_h = nc.dram_tensor("wo_h", [2, 4, 128, 1024], F32, kind="ExternalInput")
    w1_h = nc.dram_tensor("w1_h", [2, NPAIR, 8, 128, 256], F32, kind="ExternalInput")
    w2_h = nc.dram_tensor("w2_h", [2, 8, NPAIR, 128, 128], F32, kind="ExternalInput")
    b1_h = nc.dram_tensor("b1_h", [2, 128, 2 * NPAIR], F32, kind="ExternalInput")
    b2_h = nc.dram_tensor("b2_h", [2, 128, 8], F32, kind="ExternalInput")
    lng_h = nc.dram_tensor("lng_h", [2, 128, 8], F32, kind="ExternalInput")
    lnb_h = nc.dram_tensor("lnb_h", [2, 128, 8], F32, kind="ExternalInput")
    qs_d = nc.dram_tensor("qs_d", [2, 64], F32, kind="ExternalInput")
    ks_d = nc.dram_tensor("ks_d", [2, 64], F32, kind="ExternalInput")
    out_d = nc.dram_tensor("out_d", [TOWN, D], F32, kind="ExternalOutput")
    q_dram = nc.dram_tensor("q_dram", [TOWN, D], BF16)
    cc_in = nc.dram_tensor("cc_in", [CCROWS, 128], F32)
    cc_out = nc.dram_tensor("cc_out", [NCORES * CCROWS, 128], F32)
    if debug:
        dbg_res = nc.dram_tensor("dbg_res", [128, 8 * RC], F32, kind="ExternalOutput")
        dbg2 = nc.dram_tensor("dbg2", [TEXT, D], F32, kind="ExternalOutput")
        dbg3 = nc.dram_tensor("dbg3", [128, 16], F32, kind="ExternalOutput")
        dbg4 = nc.dram_tensor("dbg4", [128, 2 * 2048 + 64], F32, kind="ExternalOutput")
        dbg_kv = nc.dram_tensor("dbg_kv", [64, 1024 + 16], F32, kind="ExternalOutput")
        dbg_q = nc.dram_tensor("dbg_q", [TOWN, D], BF16, kind="ExternalOutput")

    es = contextlib.ExitStack()
    RES = es.enter_context(nc.sbuf_tensor("RES", [128, 8 * RC], F32))
    masks = es.enter_context(nc.sbuf_tensor("masks", [128, 5 * 640], BF16))
    ident_b = es.enter_context(nc.sbuf_tensor("ident_b", [128, 128], BF16))
    ident_f = es.enter_context(nc.sbuf_tensor("ident_f", [128, 128], F32))
    ones_col_b = es.enter_context(nc.sbuf_tensor("ones_col_b", [128, 1], BF16))
    ones_row_f = es.enter_context(nc.sbuf_tensor("ones_row_f", [1, 128], F32))
    qs_rep = es.enter_context(nc.sbuf_tensor("qs_rep", [128, 128], F32))
    ks_rep = es.enter_context(nc.sbuf_tensor("ks_rep", [128, 128], F32))
    bsc_rep = es.enter_context(nc.sbuf_tensor("bsc_rep", [128, 1], F32))
    kv_part = es.enter_context(nc.sbuf_tensor("kv_part", [64, 1024], F32))
    ksum_part = es.enter_context(nc.sbuf_tensor("ksum_part", [64, 16], F32))
    kvaug = es.enter_context(nc.sbuf_tensor("kvaug", [128, 16 * 65], BF16))
    w1g = [es.enter_context(nc.sbuf_tensor(f"w1g{i}", [128, 8 * 256], BF16)) for i in range(3)]
    b1_sb = es.enter_context(nc.sbuf_tensor("b1_sb", [128, 2 * NPAIR], F32))
    b2_sb = es.enter_context(nc.sbuf_tensor("b2_sb", [128, 8], F32))
    lng_sb = es.enter_context(nc.sbuf_tensor("lng_sb", [128, 8], F32))
    lnb_sb = es.enter_context(nc.sbuf_tensor("lnb_sb", [128, 8], F32))
    PSP = [es.enter_context(nc.psum_tensor(f"PSP{i}", [128, 1024], F32, bank=2 * i)) for i in range(4)]
    PS = [PSP[i // 2][:, 512 * (i % 2):512 * (i % 2) + 512] for i in range(8)]
    junk = es.enter_context(nc.sbuf_tensor("junk", [128, 2], F32))
    es_o = contextlib.ExitStack()
    wqkvo = es_o.enter_context(nc.sbuf_tensor("wqkvo", [128, 8 * 3072], BF16))

    em = Emitter(nc, es)
    _vprev = [None]

    def _span(ap):
        try:
            off = int(ap.offset)
        except Exception:
            return None
        hi = off
        for step, count in list(ap.ap):
            hi += abs(int(step)) * (int(count) - 1)
        return (ap.tensor.name, off, hi + 1)

    def _ovl(a, b):
        return a and b and a[0] == b[0] and a[1] < b[2] and b[1] < a[2]

    def _vhazard(out_ap, in_aps):
        prev = _vprev[0]
        if prev is not None and any(_ovl(_span(i), prev) for i in in_aps if i is not None):
            em.op("vector", lambda e: e.tensor_single_scalar(junk[:, 0:2], junk[:, 0:2], 1.0, ALU.mult))
        _vprev[0] = _span(out_ap)

    def finish(ret_es=True):
        es_o.close()
        if debug:
            em.op("sync", lambda e: e.dma_start(out=dbg_res.ap(), in_=RES[:, :]), inc="od0")
            em.op("sync", lambda e: e.dma_start(out=dbg_kv.ap()[:, 0:1024], in_=kv_part[0:64, :]), inc="od0")
            em.op("sync", lambda e: e.dma_start(out=dbg_kv.ap()[:, 1024:1040], in_=ksum_part[0:64, :]), inc="od0")
            if n_phases >= 2:
                em.op("sync", lambda e: e.dma_start(out=dbg_q.ap(), in_=q_dram.ap()), inc="od0")
            em.run_block("dbgdump")
        es.close()
        return nc

    def t_mm(out, lhsT, rhs, start, stop, waits=(), inc=None):
        return em.op("tensor", lambda e, o=out, l=lhsT, r=rhs, a=start, b=stop:
                     e.matmul(o, l, r, start=a, stop=b), waits, inc)

    def t_tr(out, in_, ident, waits=(), inc=None):
        return em.op("tensor", lambda e, o=out, i=in_, d=ident: e.transpose(o, i, d), waits, inc)

    def v_copy(out, in_, waits=(), inc=None):
        _vhazard(out, [in_])
        return em.op("vector", lambda e, o=out, i=in_: e.tensor_copy(o, i), waits, inc)

    def v_tt(op, out, a, b, waits=(), inc=None):
        _vhazard(out, [a, b])
        return em.op("vector", lambda e, o=out, x=a, y=b, p=op: e.tensor_tensor(o, x, y, p), waits, inc)

    def v_ts(op, out, in_, scalar, waits=(), inc=None):
        _vhazard(out, [in_, scalar if not isinstance(scalar, float) else None])
        return em.op("vector", lambda e, o=out, i=in_, s=scalar, p=op:
                     e.tensor_single_scalar(o, i, s, p), waits, inc)

    def v_stt(out, in0, scalar, in1, op0, op1, waits=(), inc=None):
        _vhazard(out, [in0, in1, scalar if not isinstance(scalar, float) else None])
        return em.op("vector", lambda e, o=out, i0=in0, s=scalar, i1=in1, p0=op0, p1=op1:
                     e.scalar_tensor_tensor(out=o, in0=i0, scalar=s, in1=i1, op0=p0, op1=p1), waits, inc)

    def v_red(op, out, in_, negate=False, waits=(), inc=None):
        _vhazard(out, [in_])
        return em.op("vector", lambda e, o=out, i=in_, p=op, n=negate:
                     e.tensor_reduce(o, i, AX.X, p, negate=n), waits, inc)

    def v_recip(out, in_, waits=(), inc=None):
        _vhazard(out, [in_])
        return em.op("vector", lambda e, o=out, i=in_: e.reciprocal(o, i), waits, inc)

    def v_memset(ap, val, waits=(), inc=None):
        return em.op("vector", lambda e, a=ap, v=val: e.memset(a, v), waits, inc)

    def s_gap():
        em.op("scalar", lambda e: e.activation(out=junk[:, 0:2], in_=junk[:, 0:2], func=AF.Copy))

    def s_act(out, in_, func, bias=0.0, scale=1.0, accum=None, waits=(), inc=None):
        return em.op("scalar", lambda e, o=out, i=in_, f=func, b=bias, s=scale, a=accum:
                     e.activation(out=o, in_=i, func=f, bias=b, scale=s, accum_out=a), waits, inc)

    def dma(eng, out, in_, waits=(), sem="d0"):
        return em.op(eng, lambda e, o=out, i=in_: e.dma_start(out=o, in_=i), waits, inc=sem)

    def res_ap(k, i0, ncols):
        return RES[:, RC * k + 1 + i0: RC * k + 1 + i0 + ncols]

    def ps_bf(i):
        b = PSP[i // 2].ap().bitcast(BF16)
        return b[:, 1024 * (i % 2):1024 * (i % 2) + 1024]

    def kpn(handle, k, n, base=0):
        return bass.AP(handle, base, [[n, 128], [128 * n, k], [1, n]])

    def A(tensor, off, dims):
        return bass.AP(tensor, off, dims)

    dma("sync", masks[:, :], kpn(masks_d, 5, 640), sem="d0")
    for l in range(2):
        dma("sync", qs_rep[:, 64 * l:64 * l + 64], qs_d.ap()[l:l + 1, :].partition_broadcast(128), sem="d0")
        dma("sync", ks_rep[:, 64 * l:64 * l + 64], ks_d.ap()[l:l + 1, :].partition_broadcast(128), sem="d0")
    dma("sync", bsc_rep[:, :], bscale_d.ap().partition_broadcast(128), sem="d0")
    for buf in (ident_b, ident_f):
        mtok = em.op("gpsimd", lambda g, b=buf: g.memset(b[:, :], 0.0), (), inc="g")
        em.op("gpsimd", lambda g, b=buf: g.affine_select(
            out=b[:, :], in_=b[:, :], compare_op=ALU.not_equal, fill=1.0, base=0,
            pattern=[[-1, 128]], channel_multiplier=1), [mtok])
    em.op("gpsimd", lambda g: g.memset(ones_col_b[:, :], 1.0))
    em.op("gpsimd", lambda g: g.memset(ones_row_f[:, :], 1.0))

    em.op("gpsimd", lambda g: g.memset(junk[:, :], 0.0))
    dtok = em.op("gpsimd", lambda g: g.memset(kv_part[0:64, :], 0.0), (), inc="g")
    dtok = em.op("gpsimd", lambda g: g.memset(ksum_part[0:64, :], 0.0), [dtok], inc="g")
    if debug:
        dtok = em.op("gpsimd", lambda g: g.memset(RES[:, :], 0.0), [dtok], inc="g")
    em.op("gpsimd", lambda g: g.memset(A(RES, 0, [[8 * RC, 128], [RC, 8], [1, 1]]), 0.0), [dtok])
    dma("gpsimd", wqkvo[:, :], kpn(wqkvo_h, 8, 3072), sem="wd0")
    em.run_block("p0")
    if n_phases < 1:
        return finish()

    with contextlib.ExitStack() as ph:
        xt = [ph.enter_context(nc.sbuf_tensor(f"xt{i}", [128, 1024], F32)) for i in range(4)]
        xnt = [ph.enter_context(nc.sbuf_tensor(f"xnt{i}", [128, 1024], F32)) for i in range(2)]
        sqs2 = [ph.enter_context(nc.sbuf_tensor(f"sq_p1{i}", [128, 1024], F32)) for i in range(2)]
        st2b = [ph.enter_context(nc.sbuf_tensor(f"st_p1{i}", [128, 8], F32)) for i in range(2)]
        xn_done = [None] * NT
        tr_done = [None] * NT
        var_done = [None] * NT
        evac = {}
        for t in range(NT):
            sqs = sqs2[t % 2]
            st = st2b[t % 2]
            xc = xt[t % 4]
            ld = dma("sync" if t % 2 == 0 else "gpsimd", xc[:, :],
                     x_sl.ap()[128 * t:128 * t + 128, :],
                     [xn_done[t - 4]] if t >= 4 else [],
                     sem=f"d{t % 4}" if t % 2 == 0 else f"gx{(t // 2) % 2}")
            sqt = s_act(sqs[:, :], xc[:, :], AF.Square, waits=[ld], inc="s")
            ssq = v_red(ALU.add, st[:, 2:3], sqs[:, :], waits=[sqt], inc="v")
            v_red(ALU.add, st[:, 0:1], xc[:, :], waits=[ld])
            v_ts(ALU.mult, st[:, 1:2], st[:, 0:1], 1.0 / 1024.0)
            v_ts(ALU.mult, st[:, 3:4], st[:, 2:3], 1.0 / 1024.0, waits=[ssq])
            v_tt(ALU.mult, st[:, 4:5], st[:, 1:2], st[:, 1:2])
            v_tt(ALU.subtract, st[:, 5:6], st[:, 3:4], st[:, 4:5])
            var = v_ts(ALU.add, st[:, 5:6], st[:, 5:6], 1e-5, inc="v")
            var_done[t] = var
            lnv = s_act(st[:, 6:7], st[:, 5:6], AF.Ln, waits=[var], inc="s")
            halfm = v_ts(ALU.mult, st[:, 6:7], st[:, 6:7], -0.5, waits=[lnv], inc="v")
            rstd = s_act(st[:, 6:7], st[:, 6:7], AF.Exp, waits=[halfm], inc="s")
            v_tt(ALU.mult, st[:, 7:8], st[:, 1:2], st[:, 6:7], waits=[rstd])
            nmr = v_ts(ALU.mult, st[:, 7:8], st[:, 7:8], -1.0, inc="v")
            xn_done[t] = s_act(xnt[t % 2][:, :], xc[:, :], AF.Identity,
                               bias=st[:, 7:8], scale=st[:, 6:7],
                               waits=[nmr] + ([tr_done[t - 2]] if t >= 2 else []), inc="s")
            if debug:
                xn_done[t] = dma("sync", dbg2.ap()[128 * t:128 * t + 128, :], xnt[t % 2][:, :],
                                 [xn_done[t]], sem=f"od{t % 2}")

            tr = None
            for k in range(8):
                w = [xn_done[t]] + ([evac.get(7)] if k == 0 and 7 in evac else [])
                tr = t_tr(PS[6 + k % 2][:, 128 * (k // 2):128 * (k // 2) + 128],
                          xnt[t % 2][:, 128 * k:128 * k + 128], ident_f[:, :],
                          w if k == 0 else (), inc="t" if k == 7 else None)
            tr_done[t] = tr
            s_act(A(RES, 1 + 128 * t, [[8 * RC, 128], [2 * RC, 4], [1, 128]]),
                  PS[6][:, 0:512], AF.Copy, waits=[tr])
            evac[7] = s_act(A(RES, RC + 1 + 128 * t, [[8 * RC, 128], [2 * RC, 4], [1, 128]]),
                            PS[7][:, 0:512], AF.Copy, inc="s")
        em.run_block("p1")
    if n_phases < 2:
        return finish()

    with contextlib.ExitStack() as ph:
        xq = [ph.enter_context(nc.sbuf_tensor(f"xq{i}", [128, 1024], BF16)) for i in range(2)]
        qk_f = ph.enter_context(nc.sbuf_tensor("qk_f2", [128, 2048], F32))
        sqv = ph.enter_context(nc.sbuf_tensor("sqv2", [128, 2048], BF16))
        ssq = ph.enter_context(nc.sbuf_tensor("ssq2", [128, 32], F32))
        qsp = ph.enter_context(nc.sbuf_tensor("qsp2", [128, 1024], BF16))
        ksp = ph.enter_context(nc.sbuf_tensor("ksp2", [128, 1024], BF16))
        vtb = ph.enter_context(nc.sbuf_tensor("vtb2", [128, 1024], BF16))
        qk_evac = v_evac = last_sp = last_vmm = last_qdma = None
        kv_mm = None
        kv_acc_tok = None
        for i in range(16):
            xq_tok = v_copy(xq[i % 2][:, :],
                            A(RES, 1 + HALO + 128 * i, [[8 * RC, 128], [RC, 8], [1, 128]]),
                            [last_vmm] if i >= 2 else [], inc="v")
            qk_mm = None
            for nb in range(4):
                w = [xq_tok]
                if i >= 1:
                    w.append(v_evac if nb < 2 else qk_evac)
                for k in range(8):
                    qk_mm = t_mm(PS[nb][:, :], xq[i % 2][:, 128 * k:128 * k + 128],
                                 wqkvo[:, 3072 * k + 512 * nb: 3072 * k + 512 * nb + 512],
                                 k == 0, k == 7, w if k == 0 else (),
                                 inc="t" if (nb == 3 and k == 7) else None)
            s_act(qk_f[:, 0:512], PS[0][:, :], AF.Copy, waits=[qk_mm])
            s_act(qk_f[:, 512:1024], PS[1][:, :], AF.Copy)
            s_act(qk_f[:, 1024:1536], PS[2][:, :], AF.Copy)
            qk_evac = s_act(qk_f[:, 1536:2048], PS[3][:, :], AF.Copy, inc="s")
            v_mm = None
            for nb in range(2):
                for k in range(8):
                    v_mm = t_mm(PS[nb][:, :], xq[i % 2][:, 128 * k:128 * k + 128],
                                wqkvo[:, 3072 * k + 2048 + 512 * nb: 3072 * k + 2048 + 512 * nb + 512],
                                k == 0, k == 7, [qk_evac] if (k == 0 and nb == 0) else (),
                                inc="t" if (nb == 1 and k == 7) else None)
            last_vmm = v_mm
            s_act(vtb[:, 0:512], PS[0][:, :], AF.Copy,
                  waits=[v_mm] + ([kv_mm] if kv_mm else []))
            v_evac = s_act(vtb[:, 512:1024], PS[1][:, :], AF.Copy, inc="s")
            v_tt(ALU.mult, sqv[:, :], qk_f[:, :], qk_f[:, :], waits=[qk_evac])
            v_red(ALU.add, ssq[:, :].unsqueeze(2), A(sqv, 0, [[2048, 128], [64, 32], [1, 64]]))
            sq_tok = v_ts(ALU.max, ssq[:, :], ssq[:, :], 1e-24, inc="v")
            s_act(ssq[:, :], ssq[:, :], AF.Ln, waits=[sq_tok])
            s_gap()
            rsq = s_act(ssq[:, :], ssq[:, :], AF.Exp, scale=-0.5, inc="s")
            sc_tok = v_tt(ALU.mult, A(qk_f, 0, [[2048, 128], [64, 32], [1, 64]]),
                          A(qk_f, 0, [[2048, 128], [64, 32], [1, 64]]),
                          ssq[:, :].unsqueeze(2).broadcast_to([128, 32, 64]),
                          [rsq], inc="v")
            s_act(qk_f[:, 1024:2048], qk_f[:, 1024:2048], AF.Exp, waits=[sc_tok])
            s_act(qk_f[:, 0:1024], qk_f[:, 0:1024], AF.Exp)
            if debug and i == 0:
                dma("sync", dbg4.ap()[:, 0:2048], qk_f[:, :], [sq_tok], sem="od1")
            last_sp = s_act(ksp[:, :], qk_f[:, 1024:2048], AF.Ln, bias=1.0,
                            waits=[kv_mm] if kv_mm else (), inc="s")
            qsp_tok = s_act(qsp[:, :], qk_f[:, 0:1024], AF.Ln, bias=1.0,
                            waits=[last_qdma] if last_qdma else (), inc="s")
            if debug and i == 0:
                dma("sync", dbg4.ap()[:, 2048 + 64:2048 + 64 + 2048], qk_f[:, :], [last_sp], sem="od1")
                dma("sync", dbg4.ap()[:, 2048:2048 + 32], ssq[:, :], [last_sp], sem="od1")
            last_qdma = dma("sync", q_dram.ap()[128 * i:128 * i + 128, :], qsp[:, :], [qsp_tok], sem="od0")
            w0 = [last_sp] + ([kv_acc_tok] if kv_acc_tok else [])
            for h in range(16):
                psd = PS[6] if h < 8 else PS[7]
                t_mm(psd[0:64, 64 * (h % 8):64 * (h % 8) + 64],
                     ksp[:, 64 * h:64 * h + 64], vtb[:, 64 * h:64 * h + 64],
                     True, True, w0 if h == 0 else ())
                kv_mm = t_mm(PS[5][0:64, h:h + 1], ksp[:, 64 * h:64 * h + 64], ones_col_b[:, :],
                             True, True, (), inc="t" if h == 15 else None)
            v_tt(ALU.add, kv_part[0:64, 0:512], kv_part[0:64, 0:512], PS[6][0:64, :], [kv_mm])
            v_tt(ALU.add, kv_part[0:64, 512:1024], kv_part[0:64, 512:1024], PS[7][0:64, :])
            kv_acc_tok = v_tt(ALU.add, ksum_part[0:64, :], ksum_part[0:64, :],
                              PS[5][0:64, 0:16], inc="v")
        v_ts(ALU.mult, kv_part[0:64, 0:512], kv_part[0:64, 0:512], 0.125, waits=[kv_acc_tok])
        v_ts(ALU.mult, kv_part[0:64, 512:1024], kv_part[0:64, 512:1024], 0.125, inc="v")
        em.run_block("p2")
    es_o.close()
    if n_phases < 3:
        return finish()

    def layer_qkv_attn(l, tail=None):
        with contextlib.ExitStack() as ph:
            INb = [ph.enter_context(nc.sbuf_tensor(f"INb{i}_{l}", [128, 8 * 129], BF16)) for i in range(2)]
            tab = [ph.enter_context(nc.sbuf_tensor(f"tab{i}_{l}", [128, 256], F32)) for i in range(2)]
            qk_f = ph.enter_context(nc.sbuf_tensor(f"qkf_{l}", [128, 1024], F32))
            t2 = ph.enter_context(nc.sbuf_tensor(f"t2_{l}", [128, 1024], F32))
            ub = ph.enter_context(nc.sbuf_tensor(f"ub_{l}", [128, 1024], F32))
            qkrot = ph.enter_context(nc.sbuf_tensor(f"qkrot_{l}", [128, 1024], BF16))
            sqv = ph.enter_context(nc.sbuf_tensor(f"sqv_{l}", [128, 1024], BF16))
            ssq = ph.enter_context(nc.sbuf_tensor(f"ssq_{l}", [128, 16], F32))
            QFP = ph.enter_context(nc.sbuf_tensor(f"QFP_{l}", [128, 4 * 128], BF16))
            KFP = ph.enter_context(nc.sbuf_tensor(f"KFP_{l}", [128, 4 * 1280], BF16))
            VT = ph.enter_context(nc.sbuf_tensor(f"VT_{l}", [128, RING * 520], BF16))
            PTb = [ph.enter_context(nc.sbuf_tensor(f"PT{i}_{l}", [128, 640], BF16)) for i in range(3)]
            PTT = [ph.enter_context(nc.sbuf_tensor(f"PTT{i}_{l}", [128, 640], BF16)) for i in range(2)]
            negm8 = ph.enter_context(nc.sbuf_tensor(f"negm8_{l}", [128, 8], F32))
            nm1 = ph.enter_context(nc.sbuf_tensor(f"nm1_{l}", [128, 1], F32))
            o_tm = ph.enter_context(nc.sbuf_tensor(f"otm_{l}", [128, 512], BF16))
            rcp = ph.enter_context(nc.sbuf_tensor(f"rcp_{l}", [128, 1], F32))
            ATT = ph.enter_context(nc.sbuf_tensor(f"ATT_{l}", [128, 4 * 2176], BF16))
            wqkv = ph.enter_context(nc.sbuf_tensor(f"wqkv_{l}", [128, 8 * 1536], BF16))
            wo_sb = ph.enter_context(nc.sbuf_tensor(f"wo_{l}", [128, 4 * 1024], BF16))

            wq_tok = dma("gpsimd", wqkv[:, :], kpn(wqkv_h, 8, 1536, base=l * 8 * 128 * 1536), sem="wd0")
            wo_tok = dma("gpsimd", wo_sb[:, :], kpn(wo_h, 4, 1024, base=l * 4 * 128 * 1024), sem="wd1")

            def mslot(t):
                return t - 4 if 5 <= t <= 8 else 0

            wo_sched = {8: [(0, 512, 7)], 12: [(512, 512, 11)], 16: [(1024, 512, 15)],
                        20: [(1536, 512, 19), (2048, 128, 20)]}

            prod_tok = [None] * NT
            attn_tok = [None] * NT
            last_exp = None
            prev_exps = [None, None]
            trPT = []
            pv_last = [None, None]
            stt2 = [None, None]
            prev_stt = None
            last_ptt_ev = None
            prev_evK = None
            last_wo_evac = None
            woev = {0: None, 1: None}

            rot_toks = [None] * NT
            for t in range(NT):
                tab_tok = dma("sync", tab[t % 2][:, :], tabs_d.ap()[128 * t:128 * t + 128, :],
                              [rot_toks[t - 2]] if t >= 2 else (), sem=f"d{t % 2}")
                w = ([prod_tok[t - 2]] if t >= 2 else []) + ([last_wo_evac] if last_wo_evac else [])
                inb = v_copy(INb[t % 2][:, :], A(RES, 128 * t, [[8 * RC, 128], [RC, 8], [1, 129]]),
                             w, inc="v")
                w = [inb, wq_tok]
                if t >= 1:
                    w.append(prod_tok[t - 1])
                if t >= 5 and attn_tok[t - 1] is not None:
                    w.append(attn_tok[t - 1])
                w.append(woev[0])
                w.append(woev[1])
                qk_done = None
                for nb in range(2):
                    for k in range(8):
                        off = 1 if k < 4 else 0
                        qk_done = t_mm(PS[nb][:, :], INb[t % 2][:, 129 * k + off: 129 * k + off + 128],
                                       wqkv[:, 1536 * k + 512 * nb: 1536 * k + 512 * nb + 512],
                                       k == 0, k == 7, w if (nb == 0 and k == 0) else (),
                                       inc="t" if (nb == 1 and k == 7) else None)
                qk0ev = s_act(qk_f[:, 0:512], PS[0][:, :], AF.Copy,
                              waits=[qk_done] + ([rot_toks[t - 1]] if t >= 1 else []), inc="s")
                qke = s_act(qk_f[:, 512:1024], PS[1][:, :], AF.Copy, inc="s")
                qkv_mm = None
                for k in range(8):
                    off = 1 if k < 4 else 0
                    qkv_mm = t_mm(PS[0][:, :], INb[t % 2][:, 129 * k + off: 129 * k + off + 128],
                                  wqkv[:, 1536 * k + 1024: 1536 * k + 1536],
                                  k == 0, k == 7, [qk0ev] if k == 0 else (),
                                  inc="t" if k == 7 else None)
                slot = t % RING
                ones_tok = v_memset(A(VT, 520 * slot + 64, [[RING * 520, 128], [65, 8], [1, 1]]), 1.0, inc="v")
                vev = s_act(A(VT, 520 * slot, [[RING * 520, 128], [65, 8], [1, 64]]),
                            A(PSP[0], 0, [[1024, 128], [64, 8], [1, 64]]), AF.Copy,
                            waits=[qkv_mm], inc="s")
                v_tt(ALU.mult, sqv[:, :], qk_f[:, :], qk_f[:, :], waits=[qke])
                v_red(ALU.add, ssq[:, :].unsqueeze(2), A(sqv, 0, [[1024, 128], [64, 16], [1, 64]]))
                sq_tok = v_ts(ALU.max, ssq[:, :], ssq[:, :], 1e-24, inc="v")
                s_act(ssq[:, :], ssq[:, :], AF.Ln, waits=[sq_tok])
                s_gap()
                rsq_tok = s_act(ssq[:, :], ssq[:, :], AF.Exp, scale=-0.5, inc="s")
                v_tt(ALU.mult, A(t2, 0, [[1024, 128], [64, 16], [1, 64]]),
                     A(qk_f, 0, [[1024, 128], [64, 16], [1, 64]]),
                     ssq[:, 0:16].unsqueeze(2).broadcast_to([128, 16, 64]), [rsq_tok])
                v_tt(ALU.mult, A(t2, 0, [[1024, 128], [64, 8], [1, 64]]),
                     A(t2, 0, [[1024, 128], [64, 8], [1, 64]]),
                     qs_rep[:, None, 64 * l:64 * l + 64].broadcast_to([128, 8, 64]))
                v_tt(ALU.mult, A(t2, 512, [[1024, 128], [64, 8], [1, 64]]),
                     A(t2, 512, [[1024, 128], [64, 8], [1, 64]]),
                     ks_rep[:, None, 64 * l:64 * l + 64].broadcast_to([128, 8, 64]))
                tb = tab[t % 2]
                v_tt(ALU.mult, A(qkrot, 0, [[1024, 128], [64, 8], [1, 64]]),
                     A(t2, 0, [[1024, 128], [64, 8], [1, 64]]),
                     tb[:, None, 0:64].broadcast_to([128, 8, 64]), waits=[tab_tok])
                v_tt(ALU.mult, A(qkrot, 512, [[1024, 128], [64, 8], [1, 64]]),
                     A(t2, 512, [[1024, 128], [64, 8], [1, 64]]),
                     tb[:, None, 128:192].broadcast_to([128, 8, 64]))
                for (qoff, toff) in ((0, 64), (512, 192)):
                    v_tt(ALU.mult, A(ub, qoff, [[1024, 128], [64, 8], [1, 32]]),
                         A(t2, qoff + 32, [[1024, 128], [64, 8], [1, 32]]),
                         tb[:, None, toff:toff + 32].broadcast_to([128, 8, 32]))
                    v_tt(ALU.mult, A(ub, qoff + 32, [[1024, 128], [64, 8], [1, 32]]),
                         A(t2, qoff, [[1024, 128], [64, 8], [1, 32]]),
                         tb[:, None, toff + 32:toff + 64].broadcast_to([128, 8, 32]))
                rot_tok = v_tt(ALU.add, qkrot[:, :], qkrot[:, :], ub[:, :], inc="v")
                rot_toks[t] = rot_tok
                for blkn in range(8):
                    w = []
                    if blkn == 0:
                        w = [rot_tok] + ([attn_tok[t - 1]] if t >= 5 and attn_tok[t - 1] else []) \
                            + ([prev_evK] if prev_evK else [])
                    tr = t_tr(ps_bf(6 + blkn // 4)[:, 128 * (blkn % 4):128 * (blkn % 4) + 128],
                              qkrot[:, 128 * blkn:128 * blkn + 128], ident_b[:, :], w,
                              inc="t" if blkn == 7 else None)
                s_act(QFP[:, 0:512], ps_bf(6)[:, 0:512], AF.Copy, waits=[tr])
                prev_evK = evK = s_act(A(KFP, 128 * slot, [[5120, 128], [1280, 4], [1, 128]]),
                                       ps_bf(7)[:, 0:512], AF.Copy, inc="s")
                if slot <= 3:
                    dupK = em.op("gpsimd", lambda g, sl_=slot: g.tensor_copy(
                        A(KFP, 128 * (sl_ + RING), [[5120, 128], [1280, 4], [1, 128]]),
                        A(KFP, 128 * sl_, [[5120, 128], [1280, 4], [1, 128]])), [evK], inc="g")
                else:
                    dupK = evK
                prod_tok[t] = em.op("scalar", lambda e: e.copy(junk[:, 0:2], junk[:, 0:2]),
                                    [dupK, vev, ones_tok], inc="s")

                if t >= 4:
                    ms = mslot(t)
                    s0 = (t - 4) % RING
                    exps = [None] * 8
                    exps1 = [None] * 8

                    def emit_scores(h):
                        p, pb = h // 2, 64 * (h % 2)
                        Be = PSP[1] if h % 2 == 0 else PSP[2]
                        if h < 2:
                            w0 = [prod_tok[t]]
                            if prev_exps[h % 2] is not None:
                                w0.append(prev_exps[h % 2])
                        else:
                            w0 = [exps[h - 2]]
                        t_mm(Be[:, 0:512], QFP[pb:pb + 64, 128 * p:128 * p + 128],
                             KFP[pb:pb + 64, 1280 * p + 128 * s0:1280 * p + 128 * s0 + 512],
                             True, False, w0)
                        t_mm(Be[:, 0:512], ident_b[:, :], masks[:, 640 * ms:640 * ms + 512],
                             False, True)
                        t_mm(Be[:, 512:640], QFP[pb:pb + 64, 128 * p:128 * p + 128],
                             KFP[pb:pb + 64, 1280 * p + 128 * s0 + 512:1280 * p + 128 * s0 + 640],
                             True, False)
                        sc_mm = t_mm(Be[:, 512:640], ident_b[:, :],
                                     masks[:, 640 * ms + 512:640 * ms + 640],
                                     False, True, (), inc="t")
                        mn = v_red(ALU.max, negm8[:, h:h + 1], Be[:, 0:640], negate=True,
                                   waits=[sc_mm], inc="v")
                        we = [mn]
                        g = len(trPT)
                        if g >= 3 and trPT[g - 3] is not None:
                            we.append(trPT[g - 3])
                        trPT.append(None)
                        exps[h] = s_act(PTb[g % 3][:, 0:640], Be[:, 0:640], AF.Exp,
                                        bias=negm8[:, h:h + 1], waits=we, inc="s")
                        return g

                    def emit_trs(h, g):
                        nonlocal last_ptt_ev
                        hp = h % 2
                        tr = None
                        for c in range(5):
                            wt = []
                            if c == 0:
                                wt = [exps[h], woev[0], woev[1]]
                                if last_ptt_ev is not None:
                                    wt.append(last_ptt_ev)
                            tr = t_tr(ps_bf(c % 2)[:, 128 * (c // 2):128 * (c // 2) + 128],
                                      PTb[g % 3][:, 128 * c:128 * c + 128],
                                      ident_b[:, :], wt, inc="t" if c == 4 else None)
                        trPT[g] = tr
                        wev = [tr]
                        if pv_last[hp] is not None:
                            wev.append(pv_last[hp])
                        s_act(A(PTT[hp], 0, [[640, 128], [256, 3], [1, 128]]),
                              ps_bf(0)[:, 0:384], AF.Copy, waits=wev)
                        last_ptt_ev = s_act(A(PTT[hp], 128, [[640, 128], [256, 2], [1, 128]]),
                                            ps_bf(1)[:, 0:256], AF.Copy, inc="s")
                        ptt_ev[h] = last_ptt_ev

                    def emit_pv(h):
                        nonlocal prev_stt
                        hp = h % 2
                        Bp = PS[7] if hp == 0 else PS[6]
                        pv = None
                        for c in range(5):
                            sl = (t - 4 + c) % RING
                            wt = ()
                            if c == 0:
                                wt = [ptt_ev[h]] + ([stt2[hp]] if stt2[hp] else [])
                            pv = t_mm(Bp[:, 0:65], PTT[hp][:, 128 * c:128 * c + 128],
                                      VT[:, 520 * sl + 65 * h: 520 * sl + 65 * h + 65],
                                      c == 0, c == 4, wt, inc="t" if c == 4 else None)
                        pv_last[hp] = pv
                        v_recip(rcp[:, :], Bp[:, 64:65], waits=[pv])
                        stt2[hp] = prev_stt = v_ts(ALU.mult, o_tm[:, 64 * h:64 * h + 64],
                                                   Bp[:, 0:64], rcp[:, :], inc="v")

                    gs = [None] * 8
                    ptt_ev = [None] * 8
                    for h in range(8):
                        gs[h] = emit_scores(h)
                        if h >= 1:
                            emit_trs(h - 1, gs[h - 1])
                        if h >= 2:
                            emit_pv(h - 2)
                    emit_trs(7, gs[7])
                    emit_pv(6)
                    emit_pv(7)
                    prev_exps = [exps[6], exps[7]]
                    last_exp = exps[7]
                    atr = None
                    for blkn in range(4):
                        w = [prev_stt, last_ptt_ev, woev[0], woev[1]] if blkn == 0 else []
                        atr = t_tr(ps_bf(blkn % 2)[:, 128 * (blkn // 2):128 * (blkn // 2) + 128],
                                   o_tm[:, 128 * blkn:128 * blkn + 128], ident_b[:, :], w,
                                   inc="t" if blkn == 3 else None)
                    s_act(A(ATT, 128 * (t - 4), [[4 * 2176, 128], [2 * 2176, 2], [1, 128]]),
                          ps_bf(0)[:, 0:256], AF.Copy, waits=[atr])
                    attn_tok[t] = s_act(A(ATT, 2176 + 128 * (t - 4), [[4 * 2176, 128], [2 * 2176, 2], [1, 128]]),
                                        ps_bf(1)[:, 0:256], AF.Copy, inc="s")

                for (c0, bl, tneed) in wo_sched.get(t, ()):
                    for oc in range(8):
                        par = oc % 2
                        w = [attn_tok[tneed], wo_tok, prod_tok[t]]
                        if attn_tok[t] is not None:
                            w.append(attn_tok[t])
                        if woev[par] is not None:
                            w.append(woev[par])
                        wo_mm = None
                        for kc in range(4):
                            wo_mm = t_mm(PS[par][:, 0:bl],
                                         wo_sb[:, 1024 * kc + 128 * oc: 1024 * kc + 128 * oc + 128],
                                         ATT[:, 2176 * kc + c0: 2176 * kc + c0 + bl],
                                         kc == 0, kc == 3, w if kc == 0 else (),
                                         inc="t" if kc == 3 else None)
                        woev[par] = v_tt(ALU.add, res_ap(oc, 512 + c0, bl),
                                         res_ap(oc, 512 + c0, bl), PS[par][:, 0:bl],
                                         [wo_mm], inc="v")
                        last_wo_evac = woev[par]
            if debug:
                for cc in range(4):
                    dma("gpsimd", dbg4.ap()[:, 512 * cc:512 * cc + 512],
                        ATT[:, 2176 * cc + 1280: 2176 * cc + 1792], [attn_tok[20]], sem="gd")
            if tail is not None:
                tail()
            em.run_block(f"attn_{l}")

    def prefetch_w1(l, waits=()):
        return [dma("gpsimd", w1g[i][:, :],
                    kpn(w1_h, 8, 256, base=(l * NPAIR + i) * 8 * 128 * 256),
                    waits if i == 0 else (), sem=f"wd{i % 3}") for i in range(3)]

    def layer_mlp(l, half, tail_oc=None, w1pre=None, prefetch_next=False):
        i0 = HALO + 1024 * half
        with contextlib.ExitStack() as ph:
            hs = ph.enter_context(nc.sbuf_tensor(f"hs_{l}_{half}", [128, 8 * 512], BF16))
            LN1 = ph.enter_context(nc.sbuf_tensor(f"LN1_{l}_{half}", [128, 8 * 1024], BF16))
            MH = ph.enter_context(nc.sbuf_tensor(f"MH_{l}_{half}", [128, NPAIR * 1024], BF16))
            w1c = w1g
            w2c = [ph.enter_context(nc.sbuf_tensor(f"w2c{i}_{l}_{half}", [128, NPAIR * 128], BF16)) for i in range(2)]
            gl = ph.enter_context(nc.sbuf_tensor(f"gl_{l}_{half}", [128, 512], BF16))
            tmp = ph.enter_context(nc.sbuf_tensor(f"tmp_{l}_{half}", [128, 512], F32))
            sp = ph.enter_context(nc.sbuf_tensor(f"sp_{l}_{half}", [1, 3 * 512], F32))

            dma("sync", b1_sb[:, :], b1_h.ap()[l], sem="d3")
            dma("sync", b2_sb[:, :], b2_h.ap()[l], sem="d3")
            dma("sync", lng_sb[:, :], lng_h.ap()[l], sem="d3")
            btok = dma("sync", lnb_sb[:, :], lnb_h.ap()[l], sem="d3")

            w1tok = [None] * NPAIR
            if w1pre is not None:
                w1tok[0:3] = w1pre
            else:
                for i in range(3):
                    w1tok[i] = dma("gpsimd", w1c[i][:, :],
                                   kpn(w1_h, 8, 256, base=(l * NPAIR + i) * 8 * 128 * 256), sem=f"wd{i % 3}")

            ln_tok = [None, None]
            prev_stats = None
            prev_lt = None
            for bk in range(2):
                j0 = i0 + 512 * bk
                v_copy(A(hs, 0, [[4096, 128], [512, 4], [1, 512]]),
                       A(RES, 1 + j0, [[8 * RC, 128], [RC, 4], [1, 512]]),
                       [prev_stats] if prev_stats else ())
                cast2 = v_copy(A(hs, 2048, [[4096, 128], [512, 4], [1, 512]]),
                               A(RES, 4 * RC + j0, [[8 * RC, 128], [RC, 4], [1, 512]]), inc="v")
                sum_mm = None
                for k in range(8):
                    sum_mm = t_mm(PS[6][0:1, 0:512], ones_col_b[:, :], hs[:, 512 * k:512 * k + 512],
                                  k == 0, k == 7,
                                  ([cast2] + ([prev_lt] if prev_lt else [])) if k == 0 else (),
                                  inc="t" if k == 7 else None)
                sq_tok = s_act(hs[:, :], hs[:, :], AF.Square, waits=[sum_mm], inc="s")
                st_mm = None
                for k in range(8):
                    st_mm = t_mm(PS[7][0:1, 0:512], ones_col_b[:, :], hs[:, 512 * k:512 * k + 512],
                                 k == 0, k == 7, [sq_tok] if k == 0 else (),
                                 inc="t" if k == 7 else None)
                prev_stats = st_mm
                v_ts(ALU.mult, sp[0:1, 0:512], PS[6][0:1, 0:512], -1.0 / 1024.0, waits=[st_mm])
                v_ts(ALU.mult, sp[0:1, 512:1024], PS[7][0:1, 0:512], 1.0 / 1024.0)
                v_tt(ALU.mult, sp[0:1, 1024:1536], sp[0:1, 0:512], sp[0:1, 0:512])
                v_tt(ALU.subtract, sp[0:1, 512:1024], sp[0:1, 512:1024],
                     sp[0:1, 1024:1536])
                var_tok = v_ts(ALU.add, sp[0:1, 512:1024], sp[0:1, 512:1024], 1e-5, inc="v")
                lnv = s_act(sp[0:1, 512:1024], sp[0:1, 512:1024], AF.Ln, waits=[var_tok], inc="s")
                halfm = v_ts(ALU.mult, sp[0:1, 512:1024], sp[0:1, 512:1024], -0.5, waits=[lnv], inc="v")
                rstd_tok = s_act(sp[0:1, 512:1024], sp[0:1, 512:1024], AF.Exp, waits=[halfm], inc="s")
                nmr_tok = v_tt(ALU.mult, sp[0:1, 1024:1536], sp[0:1, 0:512], sp[0:1, 512:1024],
                               [rstd_tok], inc="v")
                br1 = t_mm(PS[6][:, :], ones_row_f[:, :], sp[0:1, 512:1024], True, True,
                           [rstd_tok], inc="t")
                br2 = t_mm(PS[7][:, :], ones_row_f[:, :], sp[0:1, 1024:1536], True, True,
                           [nmr_tok], inc="t")
                lt = None
                for k in range(8):
                    sh_ = 1 if k >= 4 else 0
                    t1 = v_tt(ALU.mult, tmp[:, :],
                              A(RES, RC * k + 1 + j0 - sh_, [[8 * RC, 128], [1, 512]]),
                              PS[6][:, :], [br1, br2] + ([lt] if lt else []))
                    t2_ = v_tt(ALU.add, tmp[:, :], tmp[:, :], PS[7][:, :], inc="v")
                    lt = v_stt(LN1[:, 1024 * k + 512 * bk: 1024 * k + 512 * bk + 512], tmp[:, :],
                               lng_sb[:, k:k + 1], lnb_sb[:, k:k + 1].broadcast_to([128, 512]),
                               ALU.mult, ALU.add, [t2_, btok], inc="v")
                ln_tok[bk] = lt
                prev_lt = lt

            mh_tok = None
            g_mm_last = None
            for i in range(NPAIR):
                for bk in range(2):
                    par = 2 * (i % 2)
                    w = [w1tok[i], ln_tok[bk]]
                    if (i >= 2 and bk == 0) or bk == 1:
                        w.append(mh_tok)
                    a_mm = g_mm = None
                    for k in range(8):
                        a_mm = t_mm(PS[par][:, :], w1c[i % 3][:, 256 * k: 256 * k + 128],
                                    LN1[:, 1024 * k + 512 * bk: 1024 * k + 512 * bk + 512],
                                    k == 0, k == 7, w if k == 0 else ())
                        g_mm = t_mm(PS[par + 1][:, :], w1c[i % 3][:, 256 * k + 128: 256 * k + 256],
                                    LN1[:, 1024 * k + 512 * bk: 1024 * k + 512 * bk + 512],
                                    k == 0, k == 7, (), inc="t" if k == 7 else None)
                    gelu_tok = s_act(gl[:, :], PS[par + 1][:, :], AF.Gelu,
                                     bias=b1_sb[:, 2 * i + 1:2 * i + 2],
                                     waits=[g_mm, btok] + ([mh_tok] if mh_tok else []), inc="s")
                    mh_tok = v_stt(MH[:, 1024 * i + 512 * bk: 1024 * i + 512 * bk + 512],
                                   PS[par][:, :], b1_sb[:, 2 * i:2 * i + 1], gl[:, :],
                                   ALU.add, ALU.mult, [gelu_tok], inc="v")
                g_mm_last = g_mm
                if i + 3 < NPAIR:
                    w1tok[i + 3] = dma("gpsimd", w1c[(i + 3) % 3][:, :],
                                       kpn(w1_h, 8, 256, base=(l * NPAIR + i + 3) * 8 * 128 * 256),
                                       [mh_tok], sem=f"wd{(i + 3) % 3}")

            w2tok = [None] * 8
            w2tok[0] = dma("gpsimd", w2c[0][:, :], kpn(w2_h, NPAIR, 128, base=(l * 8 + 0) * NPAIR * 128 * 128), sem="wd0")
            w2tok[1] = dma("gpsimd", w2c[1][:, :], kpn(w2_h, NPAIR, 128, base=(l * 8 + 1) * NPAIR * 128 * 128), sem="wd1")
            evtok = {4: None, 5: None}
            for oc in range(8):
                for bk in range(2):
                    par = 4 + (2 * oc + bk) % 2
                    w = [w2tok[oc], mh_tok]
                    if evtok[par]:
                        w.append(evtok[par])
                    w2_mm = None
                    for k in range(NPAIR):
                        w2_mm = t_mm(PS[par][:, :], w2c[oc % 2][:, 128 * k:128 * k + 128],
                                     MH[:, 1024 * k + 512 * bk: 1024 * k + 512 * bk + 512],
                                     k == 0, k == NPAIR - 1, w if k == 0 else (),
                                     inc="t" if k == NPAIR - 1 else None)
                    evtok[par] = v_stt(res_ap(oc, i0 + 512 * bk, 512), PS[par][:, :],
                                       b2_sb[:, oc:oc + 1], res_ap(oc, i0 + 512 * bk, 512),
                                       ALU.add, ALU.add, [w2_mm, btok], inc="v")
                if tail_oc is not None:
                    tail_oc(oc, evtok[4], evtok[5])
                if oc + 2 < 8:
                    w2tok[oc + 2] = dma("gpsimd", w2c[oc % 2][:, :],
                                        kpn(w2_h, NPAIR, 128, base=(l * 8 + oc + 2) * NPAIR * 128 * 128),
                                        [evtok[4 + (2 * oc + 1) % 2]], sem=f"wd{oc % 2}")
            nxt = prefetch_w1(l, waits=[g_mm_last]) if prefetch_next else None
            em.run_block(f"mlp_{l}_{half}")
            return nxt

    def cc_stage_head():
        dma("sync", bass.AP(cc_in, 0, [[1024, 64], [1, 1024]]), kv_part[0:64, :], sem="d0")
        dma("sync", bass.AP(cc_in, 512 * 128, [[16, 64], [1, 16]]), ksum_part[0:64, :], sem="d0")

    def cc_stage_oc(oc, e4, e5):
        return dma("sync", bass.AP(cc_in, (520 + 640 * oc) * 128, [[640, 128], [1, 640]]),
                   res_ap(oc, TOWN, HALO), [e4, e5], sem="d0")

    def cc_launch(stage_tok):
        em.op("gpsimd", lambda g: g.collective_compute(
            "AllGather", ALU.bypass, replica_groups=[list(range(NCORES))],
            ins=[cc_in.ap().opt()], outs=[cc_out.ap().opt()]), [stage_tok], inc="cc")
        return ("cc", em.counts["cc"])

    def cc_recv(cctok):
        with contextlib.ExitStack() as ph:
            kv8 = ph.enter_context(nc.sbuf_tensor("kv8", [64, 8 * 1024], F32))
            ks8 = ph.enter_context(nc.sbuf_tensor("ks8", [64, 8 * 16], F32))
            ldts = []
            for r in range(NCORES):
                q, sm = (("sync", "d0") if r % 2 == 0 else ("scalar", "d2"))
                dma(q, kv8[0:64, 1024 * r:1024 * (r + 1)],
                    bass.AP(cc_out, CCROWS * 128 * r, [[1024, 64], [1, 1024]]), [cctok], sem=sm)
                ldts.append(dma(q, ks8[0:64, 16 * r:16 * (r + 1)],
                                bass.AP(cc_out, CCROWS * 128 * r + 512 * 128, [[16, 64], [1, 16]]),
                                (), sem=sm))
            ldt2 = ldts[-2]
            ldt = ldts[-1]

            res_dsts = [res_ap(k, 0, HALO) for k in range(8)]
            sems = em.sems

            def _halo(e, ks, dsts=tuple(res_dsts)):
                pid = e.partition_id()
                with e.register("off") as off, e.register("tmr") as tmr:
                    e.reg_alu(off, pid, NCORES - 1, ALU.add)
                    e.reg_alu(tmr, off, NCORES, ALU.divide)
                    e.reg_alu(tmr, tmr, NCORES, ALU.mult)
                    e.reg_alu(off, off, tmr, ALU.subtract)
                    e.reg_alu(off, off, CCROWS * 128, ALU.mult)
                    inst = None
                    for k in ks:
                        with e.register(f"offk{k}") as offk:
                            e.reg_alu(offk, off, (520 + 640 * k) * 128, ALU.add)
                            src = bass.AP(cc_out, offk, [[640, 128], [1, 640]])
                            inst = e.dma_start(out=dsts[k], in_=src)
                            inst.then_inc(sems["bd"], 16)
                    return inst

            em.counts["bd"] += 16 * 8
            halo_tok = ("bd", em.counts["bd"])
            em.op("scalar", lambda e: _halo(e, (0, 1, 2, 3)), [cctok])
            em.op("sync", lambda e: _halo(e, (4, 5, 6, 7)), [cctok])
            sc_tok = None
            for k in range(8):
                sc_tok = v_ts(ALU.mult, res_ap(k, 0, HALO), res_ap(k, 0, HALO), bsc_rep[:, :],
                              [halo_tok] if k == 0 else (), inc="v" if k == 7 else None)
            v_tt(ALU.add, kv_part[0:64, :], kv8[0:64, 0:1024], kv8[0:64, 1024:2048], [ldt, ldt2])
            for r in range(2, 8):
                v_tt(ALU.add, kv_part[0:64, :], kv_part[0:64, :], kv8[0:64, 1024 * r:1024 * (r + 1)])
            v_tt(ALU.add, ksum_part[0:64, :], ks8[0:64, 0:16], ks8[0:64, 16:32])
            for r in range(2, 8):
                v_tt(ALU.add, ksum_part[0:64, :], ksum_part[0:64, :], ks8[0:64, 16 * r:16 * (r + 1)])
            v_ts(ALU.add, ksum_part[0:64, :], ksum_part[0:64, :], 1e-6)
            v_copy(A(kvaug, 0, [[16 * 65, 64], [65, 16], [1, 64]]),
                   A(kv_part, 0, [[1024, 64], [64, 16], [1, 64]]))
            kva = v_copy(A(kvaug, 64, [[16 * 65, 64], [65, 16], [1, 1]]),
                         ksum_part[0:64, :].unsqueeze(2), inc="v")
            dup = dma("sync", kvaug[64:128, :], kvaug[0:64, :], [kva], sem="bd")
            em.run_block("cc_recv")
            return dup

    _pre = [None]
    layer_qkv_attn(0, tail=lambda: _pre.__setitem__(0, prefetch_w1(0)))
    if n_phases < 4:
        return finish()
    cc_stage_head()
    _stage = [None]

    def _tail_oc(oc, e4, e5):
        _stage[0] = cc_stage_oc(oc, e4, e5)

    pre00 = layer_mlp(0, 1, tail_oc=_tail_oc, w1pre=_pre[0], prefetch_next=True)
    if n_phases < 5:
        return finish()
    cctok = cc_launch(_stage[0])
    layer_mlp(0, 0, w1pre=pre00)
    if n_phases < 6:
        return finish()
    kv_dup = cc_recv(cctok)
    if n_phases < 7:
        return finish()
    layer_qkv_attn(1, tail=lambda: _pre.__setitem__(0, prefetch_w1(1)))
    if n_phases < 8:
        return finish()
    pre10 = layer_mlp(1, 1, w1pre=_pre[0], prefetch_next=True)
    layer_mlp(1, 0, w1pre=pre10)
    if n_phases < 9:
        return finish()

    with contextlib.ExitStack() as ph:
        xt = [ph.enter_context(nc.sbuf_tensor(f"fxt{i}", [128, 1024], F32)) for i in range(2)]
        qt = [ph.enter_context(nc.sbuf_tensor(f"fqt{i}", [128, 1024], BF16)) for i in range(2)]
        qfm = ph.enter_context(nc.sbuf_tensor("fqfm", [128, 1024], BF16))
        OUT = [ph.enter_context(nc.sbuf_tensor(f"fout{i}", [128, 1024], F32)) for i in range(2)]
        rcp = ph.enter_context(nc.sbuf_tensor("frcp", [128, 1], F32))
        ost = [None, None]
        prev_ev = None
        prev_qcp = None
        prev_mm = None
        prev_h = None
        for i in range(16):
            j0 = HALO + 128 * i
            w = [ost[i % 2]] if ost[i % 2] else []
            xl = dma("sync", xt[i % 2][:, :], x_sl.ap()[j0:j0 + 128, :], w, sem=f"d{i % 2}")
            ql = dma("scalar", qt[i % 2][:, :], q_dram.ap()[128 * i:128 * i + 128, :], w, sem=f"d{2 + i % 2}")
            tr = None
            for k in range(8):
                wt = []
                if k == 0:
                    wt = [xl] + ([prev_ev] if prev_ev else [])
                tr = t_tr(PS[k % 2][:, 128 * (k // 2):128 * (k // 2) + 128], res_ap(k, j0, 128),
                          ident_f[:, :], wt, inc="t" if k == 7 else None)
            v_tt(ALU.add, A(OUT[i % 2], 0, [[1024, 128], [256, 4], [1, 128]]),
                 PS[0][:, 0:512], A(xt[i % 2], 0, [[1024, 128], [256, 4], [1, 128]]), [tr])
            prev_ev = v_tt(ALU.add, A(OUT[i % 2], 128, [[1024, 128], [256, 4], [1, 128]]),
                           PS[1][:, 0:512], A(xt[i % 2], 128, [[1024, 128], [256, 4], [1, 128]]),
                           inc="v")
            trq = None
            for blkn in range(8):
                wt = []
                if blkn == 0:
                    wt = [ql] + ([prev_qcp] if prev_qcp else [])
                trq = t_tr(ps_bf(2 + blkn % 2)[:, 128 * (blkn // 2):128 * (blkn // 2) + 128],
                           qt[i % 2][:, 128 * blkn:128 * blkn + 128], ident_b[:, :],
                           wt, inc="t" if blkn == 7 else None)
            wq = [trq] + ([prev_mm] if prev_mm else [])
            s_act(A(qfm, 0, [[1024, 128], [256, 4], [1, 128]]),
                  ps_bf(2)[:, 0:512], AF.Copy, waits=wq)
            prev_qcp = s_act(A(qfm, 128, [[1024, 128], [256, 4], [1, 128]]),
                             ps_bf(3)[:, 0:512], AF.Copy, inc="s")
            for h in range(16):
                p, pb = h // 2, 64 * (h % 2)
                par = 4 + h % 2
                prev_mm = t_mm(PS[par][:, 0:65], qfm[pb:pb + 64, 128 * p:128 * p + 128],
                               kvaug[pb:pb + 64, 65 * h:65 * h + 65], True, True,
                               [prev_qcp, kv_dup] + ([prev_h] if prev_h else []), inc="t")
                v_recip(rcp[:, :], PS[par][:, 64:65], waits=[prev_mm])
                prev_h = v_stt(OUT[i % 2][:, 64 * h:64 * h + 64], PS[par][:, 0:64], rcp[:, :],
                               OUT[i % 2][:, 64 * h:64 * h + 64], ALU.mult, ALU.add, inc="v")
            ost[i % 2] = dma("gpsimd", out_d.ap()[128 * i:128 * i + 128, :], OUT[i % 2][:, :],
                             [prev_h], sem=f"go{i % 2}")
        em.run_block("finale")

    return finish()


def kernel(**inputs):
    sh = _prep_shared(inputs)
    in_maps = [_prep_core(inputs, sh, c) for c in range(NCORES)]
    nc = build_program()
    res = run_bass_kernel_spmd(nc, in_maps, core_ids=list(range(NCORES)))
    out = np.concatenate([res.results[c]["out_d"] for c in range(NCORES)], 0)
    return out.reshape(1, N, D)


# version 15
# speedup vs baseline: 1.0398x; 1.0247x over previous
import contextlib
import numpy as np
import ml_dtypes

import concourse.bass as bass
import concourse.mybir as mybir
from concourse.bass_utils import run_bass_kernel_spmd

F32 = mybir.dt.float32
BF16 = mybir.dt.bfloat16
AF = mybir.ActivationFunctionType
ALU = mybir.AluOpType
AX = mybir.AxisListType

NCORES = 8
N = 16384
D = 1024
TOWN = 2048
HALO = 640
TEXT = TOWN + HALO
NT = TEXT // 128
RING = 6
RC = TEXT + 1
HID = 2730
HIDP = 2816
NPAIR = 22
CCROWS = 512 + 8 + 5120
NEG = -1e30

ENGS = ("sync", "gpsimd", "vector", "scalar", "tensor")
ONE_INC = {"t", "v", "s", "g", "cc"}
SERIAL = [False]



def _prep_shared(inp):
    f32 = np.float32
    sh = {}
    sh["wqkvo_h"] = np.ascontiguousarray(inp["L0_attn_W_qkv"].T).reshape(8, 128, 3072)
    sh["wqkv_h"] = np.ascontiguousarray(inp["L0_ffn_wqkv"]).reshape(2, 8, 128, 1536)
    sh["wo_h"] = np.ascontiguousarray(inp["L0_ffn_wo"]).reshape(2, 4, 128, 1024)

    w1 = inp["L0_ffn_w1"]
    w1a = np.zeros((2, 1024, HIDP), f32); w1a[:, :, :HID] = w1[:, :, :HID]
    w1g = np.zeros((2, 1024, HIDP), f32); w1g[:, :, :HID] = w1[:, :, HID:]
    w1_h = np.zeros((2, NPAIR, 8, 128, 256), f32)
    for i in range(NPAIR):
        for k in range(8):
            w1_h[:, i, k, :, 0:128] = w1a[:, 128 * k:128 * (k + 1), 128 * i:128 * (i + 1)]
            w1_h[:, i, k, :, 128:256] = w1g[:, 128 * k:128 * (k + 1), 128 * i:128 * (i + 1)]
    sh["w1_h"] = w1_h

    w2 = inp["L0_ffn_w2"]
    w2p = np.zeros((2, HIDP, 1024), f32); w2p[:, :HID] = w2
    w2_h = np.zeros((2, 8, NPAIR, 128, 128), f32)
    for oc in range(8):
        for k in range(NPAIR):
            w2_h[:, oc, k] = w2p[:, 128 * k:128 * (k + 1), 128 * oc:128 * (oc + 1)]
    sh["w2_h"] = w2_h

    b1 = inp["L0_ffn_b1"]
    b1a = np.zeros((2, HIDP), f32); b1a[:, :HID] = b1[:, :HID]
    b1g = np.zeros((2, HIDP), f32); b1g[:, :HID] = b1[:, HID:]
    b1_h = np.zeros((2, 128, 2 * NPAIR), f32)
    for i in range(NPAIR):
        b1_h[:, :, 2 * i] = b1a[:, 128 * i:128 * (i + 1)]
        b1_h[:, :, 2 * i + 1] = b1g[:, 128 * i:128 * (i + 1)]
    sh["b1_h"] = b1_h
    sh["b2_h"] = np.ascontiguousarray(inp["L0_ffn_b2"].reshape(2, 8, 128).transpose(0, 2, 1))
    sh["lng_h"] = np.ascontiguousarray(inp["L0_ffn_ln_g"].reshape(2, 8, 128).transpose(0, 2, 1))
    sh["lnb_h"] = np.ascontiguousarray(inp["L0_ffn_ln_b"].reshape(2, 8, 128).transpose(0, 2, 1))
    sh["qs_d"] = np.ascontiguousarray(inp["L0_ffn_q_scale"])
    sh["ks_d"] = np.ascontiguousarray(inp["L0_ffn_k_scale"])
    return sh


def _rotary_tabs(t):
    f32 = np.float32
    half_idx = np.arange(0, 64, 2, dtype=f32)
    inv_freq = (1.0 / (f32(10000.0) ** (half_idx / f32(64.0)))).astype(f32)
    t = t.astype(f32)
    freqs = (t[:, None] * inv_freq[None, :]).astype(f32)
    base = ((half_idx + f32(0.4 * 64)) / f32(1.4 * 64)).astype(f32)
    power = ((t - f32(N // 2)) / f32(256.0)).astype(f32)
    scale = (base[None, :] ** power[:, None]).astype(f32)
    cos = np.cos(freqs).astype(f32)
    sin = np.sin(freqs).astype(f32)
    c64 = np.concatenate([cos, cos], 1)
    s64 = np.concatenate([sin, sin], 1)
    sc64 = np.concatenate([scale, scale], 1)
    sgn = np.ones(64, f32); sgn[:32] = -1.0
    tabs = np.empty((len(t), 256), f32)
    tabs[:, 0:64] = 8.0 * c64 * sc64
    tabs[:, 64:128] = sgn * 8.0 * s64 * sc64
    tabs[:, 128:192] = c64 / sc64
    tabs[:, 192:256] = sgn * s64 / sc64
    return tabs


def _masks_for(core):
    i = np.arange(128)[:, None]
    j = np.arange(640)[None, :]
    m0 = np.where((j >= i) & (j <= i + 512), 0.0, NEG).astype(np.float32)
    masks = np.stack([m0] * 5)
    if core == 0:
        for slot in range(1, 5):
            qb = 128 * (slot - 1)
            masks[slot] = np.where(j < 512 - qb, NEG, m0)
    return masks.astype(ml_dtypes.bfloat16)


def _prep_core(inp, sh, core):
    f32 = np.float32
    s = TOWN * core
    x = inp["L0_attn_x"].reshape(N, D)
    lo = s - HALO
    if lo >= 0:
        x_sl = np.ascontiguousarray(x[lo:s + TOWN])
    else:
        x_sl = np.concatenate([np.zeros((-lo, D), f32), x[0:s + TOWN]], 0)
    m = {"x_sl": x_sl,
         "tabs_d": _rotary_tabs(np.arange(lo, s + TOWN)),
         "masks_d": _masks_for(core),
         "bscale_d": np.array([[0.0 if core == 0 else 1.0]], f32)}
    m.update(sh)
    return m



class Emitter:
    def __init__(self, nc, es):
        self.nc = nc
        self.streams = {e: [] for e in ENGS}
        self.sems = {}
        self.counts = {}
        self.dma_sems = ("d0", "d1", "d2", "d3", "wd0", "wd1", "wd2", "bd", "od0", "od1",
                         "gd", "go0", "go1", "gx0", "gx1")
        for name in ("t", "v", "s", "g", "cc") + self.dma_sems:
            self.sems[name] = es.enter_context(nc.semaphore("sem_" + name))
            self.counts[name] = 0
        self._lastw = {}

    def op(self, eng, fn, waits=(), inc=None):
        if SERIAL[0]:
            waits = [(s, v) for s, v in self.counts.items() if v > 0]
        tok = None
        if inc is not None:
            self.counts[inc] += 1 if inc in ONE_INC else 16
            tok = (inc, self.counts[inc])
        waits = tuple((s, v) for t_ in waits if t_ is not None for (s, v) in (t_,) if v > 0)
        sems = self.sems

        def thunk(e, waits=waits, fn=fn, inc=inc, eng=eng, self=self):
            for s, v in waits:
                key = (eng, s)
                if self._lastw.get(key, -1) < v:
                    e.wait_ge(sems[s], v)
                    self._lastw[key] = v
            inst = fn(e)
            if inc is not None:
                inst.then_inc(sems[inc], 1 if inc in ONE_INC else 16)
            return inst

        self.streams[eng].append(thunk)
        return tok

    def run_block(self, name):
        for sname in self.dma_sems:
            v = self.counts[sname]
            if v > 0:
                self.streams["sync"].append(
                    lambda e, s=self.sems[sname], v=v: e.wait_ge(s, v))
        nc = self.nc
        streams = self.streams
        self._lastw = {}
        with nc.Block(name) as blk:
            for eng in ENGS:
                if not streams[eng]:
                    continue

                def make(eng):
                    thunks = streams[eng]

                    def body(e):
                        for th in thunks:
                            th(e)
                    return body
                getattr(blk, eng)(make(eng))
        self.streams = {e: [] for e in ENGS}



def build_program(n_phases=99, debug=False):
    nc = bass.Bass("TRN2", target_bir_lowering=False)

    x_sl = nc.dram_tensor("x_sl", [TEXT, D], F32, kind="ExternalInput")
    tabs_d = nc.dram_tensor("tabs_d", [TEXT, 256], F32, kind="ExternalInput")
    masks_d = nc.dram_tensor("masks_d", [5, 128, 640], BF16, kind="ExternalInput")
    bscale_d = nc.dram_tensor("bscale_d", [1, 1], F32, kind="ExternalInput")
    wqkvo_h = nc.dram_tensor("wqkvo_h", [8, 128, 3072], F32, kind="ExternalInput")
    wqkv_h = nc.dram_tensor("wqkv_h", [2, 8, 128, 1536], F32, kind="ExternalInput")
    wo_h = nc.dram_tensor("wo_h", [2, 4, 128, 1024], F32, kind="ExternalInput")
    w1_h = nc.dram_tensor("w1_h", [2, NPAIR, 8, 128, 256], F32, kind="ExternalInput")
    w2_h = nc.dram_tensor("w2_h", [2, 8, NPAIR, 128, 128], F32, kind="ExternalInput")
    b1_h = nc.dram_tensor("b1_h", [2, 128, 2 * NPAIR], F32, kind="ExternalInput")
    b2_h = nc.dram_tensor("b2_h", [2, 128, 8], F32, kind="ExternalInput")
    lng_h = nc.dram_tensor("lng_h", [2, 128, 8], F32, kind="ExternalInput")
    lnb_h = nc.dram_tensor("lnb_h", [2, 128, 8], F32, kind="ExternalInput")
    qs_d = nc.dram_tensor("qs_d", [2, 64], F32, kind="ExternalInput")
    ks_d = nc.dram_tensor("ks_d", [2, 64], F32, kind="ExternalInput")
    out_d = nc.dram_tensor("out_d", [TOWN, D], F32, kind="ExternalOutput")
    q_dram = nc.dram_tensor("q_dram", [TOWN, D], BF16)
    cc_in = nc.dram_tensor("cc_in", [CCROWS, 128], F32)
    cc_out = nc.dram_tensor("cc_out", [NCORES * CCROWS, 128], F32)
    if debug:
        dbg_res = nc.dram_tensor("dbg_res", [128, 8 * RC], F32, kind="ExternalOutput")
        dbg2 = nc.dram_tensor("dbg2", [TEXT, D], F32, kind="ExternalOutput")
        dbg3 = nc.dram_tensor("dbg3", [128, 16], F32, kind="ExternalOutput")
        dbg4 = nc.dram_tensor("dbg4", [128, 2 * 2048 + 64], F32, kind="ExternalOutput")
        dbg_kv = nc.dram_tensor("dbg_kv", [64, 1024 + 16], F32, kind="ExternalOutput")
        dbg_q = nc.dram_tensor("dbg_q", [TOWN, D], BF16, kind="ExternalOutput")

    es = contextlib.ExitStack()
    RES = es.enter_context(nc.sbuf_tensor("RES", [128, 8 * RC], F32))
    masks = es.enter_context(nc.sbuf_tensor("masks", [128, 5 * 640], BF16))
    ident_b = es.enter_context(nc.sbuf_tensor("ident_b", [128, 128], BF16))
    ident_f = es.enter_context(nc.sbuf_tensor("ident_f", [128, 128], F32))
    ones_col_b = es.enter_context(nc.sbuf_tensor("ones_col_b", [128, 1], BF16))
    ones_row_f = es.enter_context(nc.sbuf_tensor("ones_row_f", [1, 128], F32))
    qs_rep = es.enter_context(nc.sbuf_tensor("qs_rep", [128, 128], F32))
    ks_rep = es.enter_context(nc.sbuf_tensor("ks_rep", [128, 128], F32))
    bsc_rep = es.enter_context(nc.sbuf_tensor("bsc_rep", [128, 1], F32))
    kv_part = es.enter_context(nc.sbuf_tensor("kv_part", [64, 1024], F32))
    ksum_part = es.enter_context(nc.sbuf_tensor("ksum_part", [64, 16], F32))
    kvaug = es.enter_context(nc.sbuf_tensor("kvaug", [128, 16 * 65], BF16))
    b1_sb = es.enter_context(nc.sbuf_tensor("b1_sb", [128, 2 * NPAIR], F32))
    b2_sb = es.enter_context(nc.sbuf_tensor("b2_sb", [128, 8], F32))
    lng_sb = es.enter_context(nc.sbuf_tensor("lng_sb", [128, 8], F32))
    lnb_sb = es.enter_context(nc.sbuf_tensor("lnb_sb", [128, 8], F32))
    PSP = [es.enter_context(nc.psum_tensor(f"PSP{i}", [128, 1024], F32, bank=2 * i)) for i in range(4)]
    PS = [PSP[i // 2][:, 512 * (i % 2):512 * (i % 2) + 512] for i in range(8)]
    junk = es.enter_context(nc.sbuf_tensor("junk", [128, 2], F32))
    es_o = contextlib.ExitStack()
    wqkvo = es_o.enter_context(nc.sbuf_tensor("wqkvo", [128, 8 * 3072], BF16))

    em = Emitter(nc, es)
    _vprev = [None]

    def _span(ap):
        try:
            off = int(ap.offset)
        except Exception:
            return None
        hi = off
        for step, count in list(ap.ap):
            hi += abs(int(step)) * (int(count) - 1)
        return (ap.tensor.name, off, hi + 1)

    def _ovl(a, b):
        return a and b and a[0] == b[0] and a[1] < b[2] and b[1] < a[2]

    def _vhazard(out_ap, in_aps):
        prev = _vprev[0]
        if prev is not None and any(_ovl(_span(i), prev) for i in in_aps if i is not None):
            em.op("vector", lambda e: e.tensor_single_scalar(junk[:, 0:2], junk[:, 0:2], 1.0, ALU.mult))
        _vprev[0] = _span(out_ap)

    def finish(ret_es=True):
        es_o.close()
        if debug:
            em.op("sync", lambda e: e.dma_start(out=dbg_res.ap(), in_=RES[:, :]), inc="od0")
            em.op("sync", lambda e: e.dma_start(out=dbg_kv.ap()[:, 0:1024], in_=kv_part[0:64, :]), inc="od0")
            em.op("sync", lambda e: e.dma_start(out=dbg_kv.ap()[:, 1024:1040], in_=ksum_part[0:64, :]), inc="od0")
            if n_phases >= 2:
                em.op("sync", lambda e: e.dma_start(out=dbg_q.ap(), in_=q_dram.ap()), inc="od0")
            em.run_block("dbgdump")
        es.close()
        return nc

    def t_mm(out, lhsT, rhs, start, stop, waits=(), inc=None):
        return em.op("tensor", lambda e, o=out, l=lhsT, r=rhs, a=start, b=stop:
                     e.matmul(o, l, r, start=a, stop=b), waits, inc)

    def t_tr(out, in_, ident, waits=(), inc=None):
        return em.op("tensor", lambda e, o=out, i=in_, d=ident: e.transpose(o, i, d), waits, inc)

    def v_copy(out, in_, waits=(), inc=None):
        _vhazard(out, [in_])
        return em.op("vector", lambda e, o=out, i=in_: e.tensor_copy(o, i), waits, inc)

    def v_tt(op, out, a, b, waits=(), inc=None):
        _vhazard(out, [a, b])
        return em.op("vector", lambda e, o=out, x=a, y=b, p=op: e.tensor_tensor(o, x, y, p), waits, inc)

    def v_ts(op, out, in_, scalar, waits=(), inc=None):
        _vhazard(out, [in_, scalar if not isinstance(scalar, float) else None])
        return em.op("vector", lambda e, o=out, i=in_, s=scalar, p=op:
                     e.tensor_single_scalar(o, i, s, p), waits, inc)

    def v_stt(out, in0, scalar, in1, op0, op1, waits=(), inc=None):
        _vhazard(out, [in0, in1, scalar if not isinstance(scalar, float) else None])
        return em.op("vector", lambda e, o=out, i0=in0, s=scalar, i1=in1, p0=op0, p1=op1:
                     e.scalar_tensor_tensor(out=o, in0=i0, scalar=s, in1=i1, op0=p0, op1=p1), waits, inc)

    def v_red(op, out, in_, negate=False, waits=(), inc=None):
        _vhazard(out, [in_])
        return em.op("vector", lambda e, o=out, i=in_, p=op, n=negate:
                     e.tensor_reduce(o, i, AX.X, p, negate=n), waits, inc)

    def v_recip(out, in_, waits=(), inc=None):
        _vhazard(out, [in_])
        return em.op("vector", lambda e, o=out, i=in_: e.reciprocal(o, i), waits, inc)

    def v_memset(ap, val, waits=(), inc=None):
        return em.op("vector", lambda e, a=ap, v=val: e.memset(a, v), waits, inc)

    def s_gap():
        em.op("scalar", lambda e: e.activation(out=junk[:, 0:2], in_=junk[:, 0:2], func=AF.Copy))

    def s_act(out, in_, func, bias=0.0, scale=1.0, accum=None, waits=(), inc=None):
        return em.op("scalar", lambda e, o=out, i=in_, f=func, b=bias, s=scale, a=accum:
                     e.activation(out=o, in_=i, func=f, bias=b, scale=s, accum_out=a), waits, inc)

    def dma(eng, out, in_, waits=(), sem="d0"):
        return em.op(eng, lambda e, o=out, i=in_: e.dma_start(out=o, in_=i), waits, inc=sem)

    def res_ap(k, i0, ncols):
        return RES[:, RC * k + 1 + i0: RC * k + 1 + i0 + ncols]

    def ps_bf(i):
        b = PSP[i // 2].ap().bitcast(BF16)
        return b[:, 1024 * (i % 2):1024 * (i % 2) + 1024]

    def kpn(handle, k, n, base=0):
        return bass.AP(handle, base, [[n, 128], [128 * n, k], [1, n]])

    def A(tensor, off, dims):
        return bass.AP(tensor, off, dims)

    dma("sync", masks[:, :], kpn(masks_d, 5, 640), sem="d0")
    for l in range(2):
        dma("sync", qs_rep[:, 64 * l:64 * l + 64], qs_d.ap()[l:l + 1, :].partition_broadcast(128), sem="d0")
        dma("sync", ks_rep[:, 64 * l:64 * l + 64], ks_d.ap()[l:l + 1, :].partition_broadcast(128), sem="d0")
    dma("sync", bsc_rep[:, :], bscale_d.ap().partition_broadcast(128), sem="d0")
    for buf in (ident_b, ident_f):
        mtok = em.op("gpsimd", lambda g, b=buf: g.memset(b[:, :], 0.0), (), inc="g")
        em.op("gpsimd", lambda g, b=buf: g.affine_select(
            out=b[:, :], in_=b[:, :], compare_op=ALU.not_equal, fill=1.0, base=0,
            pattern=[[-1, 128]], channel_multiplier=1), [mtok])
    em.op("gpsimd", lambda g: g.memset(ones_col_b[:, :], 1.0))
    em.op("gpsimd", lambda g: g.memset(ones_row_f[:, :], 1.0))

    em.op("gpsimd", lambda g: g.memset(junk[:, :], 0.0))
    dtok = em.op("gpsimd", lambda g: g.memset(kv_part[0:64, :], 0.0), (), inc="g")
    dtok = em.op("gpsimd", lambda g: g.memset(ksum_part[0:64, :], 0.0), [dtok], inc="g")
    if debug:
        dtok = em.op("gpsimd", lambda g: g.memset(RES[:, :], 0.0), [dtok], inc="g")
    em.op("gpsimd", lambda g: g.memset(A(RES, 0, [[8 * RC, 128], [RC, 8], [1, 1]]), 0.0), [dtok])
    em.run_block("p0")
    if n_phases < 1:
        return finish()

    with contextlib.ExitStack() as ph:
        xt = [ph.enter_context(nc.sbuf_tensor(f"xt{i}", [128, 1024], F32)) for i in range(4)]
        xnt = [ph.enter_context(nc.sbuf_tensor(f"xnt{i}", [128, 1024], F32)) for i in range(2)]
        sqs2 = [ph.enter_context(nc.sbuf_tensor(f"sq_p1{i}", [128, 1024], F32)) for i in range(2)]
        st2b = [ph.enter_context(nc.sbuf_tensor(f"st_p1{i}", [128, 8], F32)) for i in range(2)]
        xn_done = [None] * NT
        tr_done = [None] * NT
        var_done = [None] * NT
        evac = {}
        for t in range(NT):
            sqs = sqs2[t % 2]
            st = st2b[t % 2]
            xc = xt[t % 4]
            ld = dma("sync" if t % 2 == 0 else "gpsimd", xc[:, :],
                     x_sl.ap()[128 * t:128 * t + 128, :],
                     [xn_done[t - 4]] if t >= 4 else [],
                     sem=f"d{t % 4}" if t % 2 == 0 else f"gx{(t // 2) % 2}")
            sqt = s_act(sqs[:, :], xc[:, :], AF.Square, waits=[ld], inc="s")
            ssq = v_red(ALU.add, st[:, 2:3], sqs[:, :], waits=[sqt], inc="v")
            v_red(ALU.add, st[:, 0:1], xc[:, :], waits=[ld])
            v_ts(ALU.mult, st[:, 1:2], st[:, 0:1], 1.0 / 1024.0)
            v_ts(ALU.mult, st[:, 3:4], st[:, 2:3], 1.0 / 1024.0, waits=[ssq])
            v_tt(ALU.mult, st[:, 4:5], st[:, 1:2], st[:, 1:2])
            v_tt(ALU.subtract, st[:, 5:6], st[:, 3:4], st[:, 4:5])
            var = v_ts(ALU.add, st[:, 5:6], st[:, 5:6], 1e-5, inc="v")
            var_done[t] = var
            lnv = s_act(st[:, 6:7], st[:, 5:6], AF.Ln, waits=[var], inc="s")
            halfm = v_ts(ALU.mult, st[:, 6:7], st[:, 6:7], -0.5, waits=[lnv], inc="v")
            rstd = s_act(st[:, 6:7], st[:, 6:7], AF.Exp, waits=[halfm], inc="s")
            v_tt(ALU.mult, st[:, 7:8], st[:, 1:2], st[:, 6:7], waits=[rstd])
            nmr = v_ts(ALU.mult, st[:, 7:8], st[:, 7:8], -1.0, inc="v")
            xn_done[t] = s_act(xnt[t % 2][:, :], xc[:, :], AF.Identity,
                               bias=st[:, 7:8], scale=st[:, 6:7],
                               waits=[nmr] + ([tr_done[t - 2]] if t >= 2 else []), inc="s")
            if debug:
                xn_done[t] = dma("sync", dbg2.ap()[128 * t:128 * t + 128, :], xnt[t % 2][:, :],
                                 [xn_done[t]], sem=f"od{t % 2}")

            tr = None
            for k in range(8):
                w = [xn_done[t]] + ([evac.get(7)] if k == 0 and 7 in evac else [])
                tr = t_tr(PS[6 + k % 2][:, 128 * (k // 2):128 * (k // 2) + 128],
                          xnt[t % 2][:, 128 * k:128 * k + 128], ident_f[:, :],
                          w if k == 0 else (), inc="t" if k == 7 else None)
            tr_done[t] = tr
            s_act(A(RES, 1 + 128 * t, [[8 * RC, 128], [2 * RC, 4], [1, 128]]),
                  PS[6][:, 0:512], AF.Copy, waits=[tr])
            evac[7] = s_act(A(RES, RC + 1 + 128 * t, [[8 * RC, 128], [2 * RC, 4], [1, 128]]),
                            PS[7][:, 0:512], AF.Copy, inc="s")
        dma("gpsimd", wqkvo[:, :], kpn(wqkvo_h, 8, 3072), sem="wd0")
        em.run_block("p1")
    if n_phases < 2:
        return finish()

    with contextlib.ExitStack() as ph:
        xq = [ph.enter_context(nc.sbuf_tensor(f"xq{i}", [128, 1024], BF16)) for i in range(2)]
        qk_f = ph.enter_context(nc.sbuf_tensor("qk_f2", [128, 2048], F32))
        sqv = ph.enter_context(nc.sbuf_tensor("sqv2", [128, 2048], BF16))
        ssq = ph.enter_context(nc.sbuf_tensor("ssq2", [128, 32], F32))
        qsp = ph.enter_context(nc.sbuf_tensor("qsp2", [128, 1024], BF16))
        ksp = ph.enter_context(nc.sbuf_tensor("ksp2", [128, 1024], BF16))
        vtb = ph.enter_context(nc.sbuf_tensor("vtb2", [128, 1024], BF16))
        qk_evac = v_evac = last_sp = last_vmm = last_qdma = None
        kv_mm = None
        kv_acc_tok = None
        for i in range(16):
            xq_tok = v_copy(xq[i % 2][:, :],
                            A(RES, 1 + HALO + 128 * i, [[8 * RC, 128], [RC, 8], [1, 128]]),
                            [last_vmm] if i >= 2 else [], inc="v")
            qk_mm = None
            for nb in range(4):
                w = [xq_tok]
                if i >= 1:
                    w.append(v_evac if nb < 2 else qk_evac)
                for k in range(8):
                    qk_mm = t_mm(PS[nb][:, :], xq[i % 2][:, 128 * k:128 * k + 128],
                                 wqkvo[:, 3072 * k + 512 * nb: 3072 * k + 512 * nb + 512],
                                 k == 0, k == 7, w if k == 0 else (),
                                 inc="t" if (nb == 3 and k == 7) else None)
            s_act(qk_f[:, 0:512], PS[0][:, :], AF.Copy, waits=[qk_mm])
            s_act(qk_f[:, 512:1024], PS[1][:, :], AF.Copy)
            s_act(qk_f[:, 1024:1536], PS[2][:, :], AF.Copy)
            qk_evac = s_act(qk_f[:, 1536:2048], PS[3][:, :], AF.Copy, inc="s")
            v_mm = None
            for nb in range(2):
                for k in range(8):
                    v_mm = t_mm(PS[nb][:, :], xq[i % 2][:, 128 * k:128 * k + 128],
                                wqkvo[:, 3072 * k + 2048 + 512 * nb: 3072 * k + 2048 + 512 * nb + 512],
                                k == 0, k == 7, [qk_evac] if (k == 0 and nb == 0) else (),
                                inc="t" if (nb == 1 and k == 7) else None)
            last_vmm = v_mm
            s_act(vtb[:, 0:512], PS[0][:, :], AF.Copy,
                  waits=[v_mm] + ([kv_mm] if kv_mm else []))
            v_evac = s_act(vtb[:, 512:1024], PS[1][:, :], AF.Copy, inc="s")
            v_tt(ALU.mult, sqv[:, :], qk_f[:, :], qk_f[:, :], waits=[qk_evac])
            v_red(ALU.add, ssq[:, :].unsqueeze(2), A(sqv, 0, [[2048, 128], [64, 32], [1, 64]]))
            sq_tok = v_ts(ALU.max, ssq[:, :], ssq[:, :], 1e-24, inc="v")
            s_act(ssq[:, :], ssq[:, :], AF.Ln, waits=[sq_tok])
            s_gap()
            rsq = s_act(ssq[:, :], ssq[:, :], AF.Exp, scale=-0.5, inc="s")
            sc_tok = v_tt(ALU.mult, A(qk_f, 0, [[2048, 128], [64, 32], [1, 64]]),
                          A(qk_f, 0, [[2048, 128], [64, 32], [1, 64]]),
                          ssq[:, :].unsqueeze(2).broadcast_to([128, 32, 64]),
                          [rsq], inc="v")
            s_act(qk_f[:, 1024:2048], qk_f[:, 1024:2048], AF.Exp, waits=[sc_tok])
            s_act(qk_f[:, 0:1024], qk_f[:, 0:1024], AF.Exp)
            if debug and i == 0:
                dma("sync", dbg4.ap()[:, 0:2048], qk_f[:, :], [sq_tok], sem="od1")
            last_sp = s_act(ksp[:, :], qk_f[:, 1024:2048], AF.Ln, bias=1.0,
                            waits=[kv_mm] if kv_mm else (), inc="s")
            qsp_tok = s_act(qsp[:, :], qk_f[:, 0:1024], AF.Ln, bias=1.0,
                            waits=[last_qdma] if last_qdma else (), inc="s")
            if debug and i == 0:
                dma("sync", dbg4.ap()[:, 2048 + 64:2048 + 64 + 2048], qk_f[:, :], [last_sp], sem="od1")
                dma("sync", dbg4.ap()[:, 2048:2048 + 32], ssq[:, :], [last_sp], sem="od1")
            last_qdma = dma("sync", q_dram.ap()[128 * i:128 * i + 128, :], qsp[:, :], [qsp_tok], sem="od0")
            w0 = [last_sp] + ([kv_acc_tok] if kv_acc_tok else [])
            for h in range(16):
                psd = PS[6] if h < 8 else PS[7]
                t_mm(psd[0:64, 64 * (h % 8):64 * (h % 8) + 64],
                     ksp[:, 64 * h:64 * h + 64], vtb[:, 64 * h:64 * h + 64],
                     True, True, w0 if h == 0 else ())
                kv_mm = t_mm(PS[5][0:64, h:h + 1], ksp[:, 64 * h:64 * h + 64], ones_col_b[:, :],
                             True, True, (), inc="t" if h == 15 else None)
            v_tt(ALU.add, kv_part[0:64, 0:512], kv_part[0:64, 0:512], PS[6][0:64, :], [kv_mm])
            v_tt(ALU.add, kv_part[0:64, 512:1024], kv_part[0:64, 512:1024], PS[7][0:64, :])
            kv_acc_tok = v_tt(ALU.add, ksum_part[0:64, :], ksum_part[0:64, :],
                              PS[5][0:64, 0:16], inc="v")
        v_ts(ALU.mult, kv_part[0:64, 0:512], kv_part[0:64, 0:512], 0.125, waits=[kv_acc_tok])
        v_ts(ALU.mult, kv_part[0:64, 512:1024], kv_part[0:64, 512:1024], 0.125, inc="v")
        em.run_block("p2")
    es_o.close()
    if n_phases < 3:
        return finish()

    def layer_qkv_attn(l):
        with contextlib.ExitStack() as ph:
            INb = [ph.enter_context(nc.sbuf_tensor(f"INb{i}_{l}", [128, 8 * 129], BF16)) for i in range(2)]
            tab = [ph.enter_context(nc.sbuf_tensor(f"tab{i}_{l}", [128, 256], F32)) for i in range(2)]
            qk_f = ph.enter_context(nc.sbuf_tensor(f"qkf_{l}", [128, 1024], F32))
            t2 = ph.enter_context(nc.sbuf_tensor(f"t2_{l}", [128, 1024], F32))
            p1b = ph.enter_context(nc.sbuf_tensor(f"p1b_{l}", [128, 1024], F32))
            ub = ph.enter_context(nc.sbuf_tensor(f"ub_{l}", [128, 1024], F32))
            qkrot = ph.enter_context(nc.sbuf_tensor(f"qkrot_{l}", [128, 1024], BF16))
            sqv = ph.enter_context(nc.sbuf_tensor(f"sqv_{l}", [128, 1024], BF16))
            ssq = ph.enter_context(nc.sbuf_tensor(f"ssq_{l}", [128, 16], F32))
            QFP = ph.enter_context(nc.sbuf_tensor(f"QFP_{l}", [128, 4 * 128], BF16))
            KFP = ph.enter_context(nc.sbuf_tensor(f"KFP_{l}", [128, 4 * 2 * RING * 128], BF16))
            VT = ph.enter_context(nc.sbuf_tensor(f"VT_{l}", [128, RING * 520], BF16))
            PTb = [ph.enter_context(nc.sbuf_tensor(f"PT{i}_{l}", [128, 640], BF16)) for i in range(3)]
            PTT = [ph.enter_context(nc.sbuf_tensor(f"PTT{i}_{l}", [128, 640], BF16)) for i in range(2)]
            negm8 = ph.enter_context(nc.sbuf_tensor(f"negm8_{l}", [128, 8], F32))
            nm1 = ph.enter_context(nc.sbuf_tensor(f"nm1_{l}", [128, 1], F32))
            o_tm = ph.enter_context(nc.sbuf_tensor(f"otm_{l}", [128, 512], BF16))
            rcp = ph.enter_context(nc.sbuf_tensor(f"rcp_{l}", [128, 1], F32))
            ATT = ph.enter_context(nc.sbuf_tensor(f"ATT_{l}", [128, 4 * 2176], BF16))
            wqkv = ph.enter_context(nc.sbuf_tensor(f"wqkv_{l}", [128, 8 * 1536], BF16))
            wo_sb = ph.enter_context(nc.sbuf_tensor(f"wo_{l}", [128, 4 * 1024], BF16))

            wq_tok = dma("gpsimd", wqkv[:, :], kpn(wqkv_h, 8, 1536, base=l * 8 * 128 * 1536), sem="wd0")
            wo_tok = dma("gpsimd", wo_sb[:, :], kpn(wo_h, 4, 1024, base=l * 4 * 128 * 1024), sem="wd1")

            def mslot(t):
                return t - 4 if 5 <= t <= 8 else 0

            wo_sched = {8: [(0, 512, 7)], 12: [(512, 512, 11)], 16: [(1024, 512, 15)],
                        20: [(1536, 512, 19), (2048, 128, 20)]}

            prod_tok = [None] * NT
            attn_tok = [None] * NT
            last_exp = None
            prev_exps = [None, None]
            trPT = []
            pv_last = [None, None]
            stt2 = [None, None]
            prev_stt = None
            last_ptt_ev = None
            prev_evK = None
            last_wo_evac = None
            woev = {0: None, 1: None}

            rot_toks = [None] * NT
            for t in range(NT):
                tab_tok = dma("sync", tab[t % 2][:, :], tabs_d.ap()[128 * t:128 * t + 128, :],
                              [rot_toks[t - 2]] if t >= 2 else (), sem=f"d{t % 2}")
                w = ([prod_tok[t - 2]] if t >= 2 else []) + ([last_wo_evac] if last_wo_evac else [])
                inb = v_copy(INb[t % 2][:, :], A(RES, 128 * t, [[8 * RC, 128], [RC, 8], [1, 129]]),
                             w, inc="v")
                w = [inb, wq_tok]
                if t >= 1:
                    w.append(prod_tok[t - 1])
                if t >= 5 and attn_tok[t - 1] is not None:
                    w.append(attn_tok[t - 1])
                w.append(woev[0])
                w.append(woev[1])
                qk_done = None
                for nb in range(2):
                    for k in range(8):
                        off = 1 if k < 4 else 0
                        qk_done = t_mm(PS[nb][:, :], INb[t % 2][:, 129 * k + off: 129 * k + off + 128],
                                       wqkv[:, 1536 * k + 512 * nb: 1536 * k + 512 * nb + 512],
                                       k == 0, k == 7, w if (nb == 0 and k == 0) else (),
                                       inc="t" if (nb == 1 and k == 7) else None)
                qk0ev = s_act(qk_f[:, 0:512], PS[0][:, :], AF.Copy,
                              waits=[qk_done] + ([rot_toks[t - 1]] if t >= 1 else []), inc="s")
                qke = s_act(qk_f[:, 512:1024], PS[1][:, :], AF.Copy, inc="s")
                qkv_mm = None
                for k in range(8):
                    off = 1 if k < 4 else 0
                    qkv_mm = t_mm(PS[0][:, :], INb[t % 2][:, 129 * k + off: 129 * k + off + 128],
                                  wqkv[:, 1536 * k + 1024: 1536 * k + 1536],
                                  k == 0, k == 7, [qk0ev] if k == 0 else (),
                                  inc="t" if k == 7 else None)
                slot = t % RING
                ones_tok = v_memset(A(VT, 520 * slot + 64, [[RING * 520, 128], [65, 8], [1, 1]]), 1.0, inc="v")
                vev = s_act(A(VT, 520 * slot, [[RING * 520, 128], [65, 8], [1, 64]]),
                            A(PSP[0], 0, [[1024, 128], [64, 8], [1, 64]]), AF.Copy,
                            waits=[qkv_mm], inc="s")
                v_tt(ALU.mult, sqv[:, :], qk_f[:, :], qk_f[:, :], waits=[qke])
                v_red(ALU.add, ssq[:, :].unsqueeze(2), A(sqv, 0, [[1024, 128], [64, 16], [1, 64]]))
                sq_tok = v_ts(ALU.max, ssq[:, :], ssq[:, :], 1e-24, inc="v")
                s_act(ssq[:, :], ssq[:, :], AF.Ln, waits=[sq_tok])
                s_gap()
                rsq_tok = s_act(ssq[:, :], ssq[:, :], AF.Exp, scale=-0.5, inc="s")
                v_tt(ALU.mult, A(t2, 0, [[1024, 128], [64, 16], [1, 64]]),
                     A(qk_f, 0, [[1024, 128], [64, 16], [1, 64]]),
                     ssq[:, 0:16].unsqueeze(2).broadcast_to([128, 16, 64]), [rsq_tok])
                v_tt(ALU.mult, A(t2, 0, [[1024, 128], [64, 8], [1, 64]]),
                     A(t2, 0, [[1024, 128], [64, 8], [1, 64]]),
                     qs_rep[:, None, 64 * l:64 * l + 64].broadcast_to([128, 8, 64]))
                v_tt(ALU.mult, A(t2, 512, [[1024, 128], [64, 8], [1, 64]]),
                     A(t2, 512, [[1024, 128], [64, 8], [1, 64]]),
                     ks_rep[:, None, 64 * l:64 * l + 64].broadcast_to([128, 8, 64]))
                tb = tab[t % 2]
                v_tt(ALU.mult, A(p1b, 0, [[1024, 128], [64, 8], [1, 64]]),
                     A(t2, 0, [[1024, 128], [64, 8], [1, 64]]),
                     tb[:, None, 0:64].broadcast_to([128, 8, 64]), waits=[tab_tok])
                v_tt(ALU.mult, A(p1b, 512, [[1024, 128], [64, 8], [1, 64]]),
                     A(t2, 512, [[1024, 128], [64, 8], [1, 64]]),
                     tb[:, None, 128:192].broadcast_to([128, 8, 64]))
                for (qoff, toff) in ((0, 64), (512, 192)):
                    v_tt(ALU.mult, A(ub, qoff, [[1024, 128], [64, 8], [1, 32]]),
                         A(t2, qoff + 32, [[1024, 128], [64, 8], [1, 32]]),
                         tb[:, None, toff:toff + 32].broadcast_to([128, 8, 32]))
                    v_tt(ALU.mult, A(ub, qoff + 32, [[1024, 128], [64, 8], [1, 32]]),
                         A(t2, qoff, [[1024, 128], [64, 8], [1, 32]]),
                         tb[:, None, toff + 32:toff + 64].broadcast_to([128, 8, 32]))
                rot_tok = v_tt(ALU.add, qkrot[:, :], p1b[:, :], ub[:, :], inc="v")
                rot_toks[t] = rot_tok
                for blkn in range(8):
                    w = []
                    if blkn == 0:
                        w = [rot_tok] + ([attn_tok[t - 1]] if t >= 5 and attn_tok[t - 1] else []) \
                            + ([prev_evK] if prev_evK else [])
                    tr = t_tr(ps_bf(6 + blkn // 4)[:, 128 * (blkn % 4):128 * (blkn % 4) + 128],
                              qkrot[:, 128 * blkn:128 * blkn + 128], ident_b[:, :], w,
                              inc="t" if blkn == 7 else None)
                s_act(QFP[:, 0:512], ps_bf(6)[:, 0:512], AF.Copy, waits=[tr])
                prev_evK = evK = s_act(A(KFP, 128 * slot, [[6144, 128], [1536, 4], [1, 128]]),
                                       ps_bf(7)[:, 0:512], AF.Copy, inc="s")
                dupK = em.op("gpsimd", lambda g, sl_=slot: g.tensor_copy(
                    A(KFP, 128 * (sl_ + RING), [[6144, 128], [1536, 4], [1, 128]]),
                    A(KFP, 128 * sl_, [[6144, 128], [1536, 4], [1, 128]])), [evK], inc="g")
                prod_tok[t] = em.op("scalar", lambda e: e.copy(junk[:, 0:2], junk[:, 0:2]),
                                    [dupK, vev, ones_tok], inc="s")

                if t >= 4:
                    ms = mslot(t)
                    s0 = (t - 4) % RING
                    exps = [None] * 8
                    exps1 = [None] * 8

                    def emit_scores(h):
                        p, pb = h // 2, 64 * (h % 2)
                        Be = PSP[1] if h % 2 == 0 else PSP[2]
                        if h < 2:
                            w0 = [prod_tok[t]]
                            if prev_exps[h % 2] is not None:
                                w0.append(prev_exps[h % 2])
                        else:
                            w0 = [exps[h - 2]]
                        t_mm(Be[:, 0:512], QFP[pb:pb + 64, 128 * p:128 * p + 128],
                             KFP[pb:pb + 64, 1536 * p + 128 * s0:1536 * p + 128 * s0 + 512],
                             True, False, w0)
                        t_mm(Be[:, 0:512], ident_b[:, :], masks[:, 640 * ms:640 * ms + 512],
                             False, True)
                        t_mm(Be[:, 512:640], QFP[pb:pb + 64, 128 * p:128 * p + 128],
                             KFP[pb:pb + 64, 1536 * p + 128 * s0 + 512:1536 * p + 128 * s0 + 640],
                             True, False)
                        sc_mm = t_mm(Be[:, 512:640], ident_b[:, :],
                                     masks[:, 640 * ms + 512:640 * ms + 640],
                                     False, True, (), inc="t")
                        mn = v_red(ALU.max, negm8[:, h:h + 1], Be[:, 0:640], negate=True,
                                   waits=[sc_mm], inc="v")
                        we = [mn]
                        g = len(trPT)
                        if g >= 3 and trPT[g - 3] is not None:
                            we.append(trPT[g - 3])
                        trPT.append(None)
                        exps[h] = s_act(PTb[g % 3][:, 0:640], Be[:, 0:640], AF.Exp,
                                        bias=negm8[:, h:h + 1], waits=we, inc="s")
                        return g

                    def emit_trs(h, g):
                        nonlocal last_ptt_ev
                        hp = h % 2
                        tr = None
                        for c in range(5):
                            wt = []
                            if c == 0:
                                wt = [exps[h], woev[0], woev[1]]
                                if last_ptt_ev is not None:
                                    wt.append(last_ptt_ev)
                            tr = t_tr(ps_bf(c % 2)[:, 128 * (c // 2):128 * (c // 2) + 128],
                                      PTb[g % 3][:, 128 * c:128 * c + 128],
                                      ident_b[:, :], wt, inc="t" if c == 4 else None)
                        trPT[g] = tr
                        wev = [tr]
                        if pv_last[hp] is not None:
                            wev.append(pv_last[hp])
                        s_act(A(PTT[hp], 0, [[640, 128], [256, 3], [1, 128]]),
                              ps_bf(0)[:, 0:384], AF.Copy, waits=wev)
                        last_ptt_ev = s_act(A(PTT[hp], 128, [[640, 128], [256, 2], [1, 128]]),
                                            ps_bf(1)[:, 0:256], AF.Copy, inc="s")
                        ptt_ev[h] = last_ptt_ev

                    def emit_pv(h):
                        nonlocal prev_stt
                        hp = h % 2
                        Bp = PS[7] if hp == 0 else PS[6]
                        pv = None
                        for c in range(5):
                            sl = (t - 4 + c) % RING
                            wt = ()
                            if c == 0:
                                wt = [ptt_ev[h]] + ([stt2[hp]] if stt2[hp] else [])
                            pv = t_mm(Bp[:, 0:65], PTT[hp][:, 128 * c:128 * c + 128],
                                      VT[:, 520 * sl + 65 * h: 520 * sl + 65 * h + 65],
                                      c == 0, c == 4, wt, inc="t" if c == 4 else None)
                        pv_last[hp] = pv
                        v_recip(rcp[:, :], Bp[:, 64:65], waits=[pv])
                        stt2[hp] = prev_stt = v_ts(ALU.mult, o_tm[:, 64 * h:64 * h + 64],
                                                   Bp[:, 0:64], rcp[:, :], inc="v")

                    gs = [None] * 8
                    ptt_ev = [None] * 8
                    for h in range(8):
                        gs[h] = emit_scores(h)
                        if h >= 1:
                            emit_trs(h - 1, gs[h - 1])
                        if h >= 2:
                            emit_pv(h - 2)
                    emit_trs(7, gs[7])
                    emit_pv(6)
                    emit_pv(7)
                    prev_exps = [exps[6], exps[7]]
                    last_exp = exps[7]
                    atr = None
                    for blkn in range(4):
                        w = [prev_stt, last_ptt_ev, woev[0], woev[1]] if blkn == 0 else []
                        atr = t_tr(ps_bf(blkn % 2)[:, 128 * (blkn // 2):128 * (blkn // 2) + 128],
                                   o_tm[:, 128 * blkn:128 * blkn + 128], ident_b[:, :], w,
                                   inc="t" if blkn == 3 else None)
                    s_act(A(ATT, 128 * (t - 4), [[4 * 2176, 128], [2 * 2176, 2], [1, 128]]),
                          ps_bf(0)[:, 0:256], AF.Copy, waits=[atr])
                    attn_tok[t] = s_act(A(ATT, 2176 + 128 * (t - 4), [[4 * 2176, 128], [2 * 2176, 2], [1, 128]]),
                                        ps_bf(1)[:, 0:256], AF.Copy, inc="s")

                for (c0, bl, tneed) in wo_sched.get(t, ()):
                    for oc in range(8):
                        par = oc % 2
                        w = [attn_tok[tneed], wo_tok, prod_tok[t]]
                        if attn_tok[t] is not None:
                            w.append(attn_tok[t])
                        if woev[par] is not None:
                            w.append(woev[par])
                        wo_mm = None
                        for kc in range(4):
                            wo_mm = t_mm(PS[par][:, 0:bl],
                                         wo_sb[:, 1024 * kc + 128 * oc: 1024 * kc + 128 * oc + 128],
                                         ATT[:, 2176 * kc + c0: 2176 * kc + c0 + bl],
                                         kc == 0, kc == 3, w if kc == 0 else (),
                                         inc="t" if kc == 3 else None)
                        woev[par] = v_tt(ALU.add, res_ap(oc, 512 + c0, bl),
                                         res_ap(oc, 512 + c0, bl), PS[par][:, 0:bl],
                                         [wo_mm], inc="v")
                        last_wo_evac = woev[par]
            if debug:
                for cc in range(4):
                    dma("gpsimd", dbg4.ap()[:, 512 * cc:512 * cc + 512],
                        ATT[:, 2176 * cc + 1280: 2176 * cc + 1792], [attn_tok[20]], sem="gd")
            em.run_block(f"attn_{l}")

    def layer_mlp(l, half, tail_oc=None):
        i0 = HALO + 1024 * half
        with contextlib.ExitStack() as ph:
            hs = ph.enter_context(nc.sbuf_tensor(f"hs_{l}_{half}", [128, 8 * 512], BF16))
            LN1 = ph.enter_context(nc.sbuf_tensor(f"LN1_{l}_{half}", [128, 8 * 1024], BF16))
            MH = ph.enter_context(nc.sbuf_tensor(f"MH_{l}_{half}", [128, NPAIR * 1024], BF16))
            w1c = [ph.enter_context(nc.sbuf_tensor(f"w1c{i}_{l}_{half}", [128, 8 * 256], BF16)) for i in range(3)]
            w2c = [ph.enter_context(nc.sbuf_tensor(f"w2c{i}_{l}_{half}", [128, NPAIR * 128], BF16)) for i in range(2)]
            gl = ph.enter_context(nc.sbuf_tensor(f"gl_{l}_{half}", [128, 512], BF16))
            tmp = ph.enter_context(nc.sbuf_tensor(f"tmp_{l}_{half}", [128, 512], F32))
            sp = ph.enter_context(nc.sbuf_tensor(f"sp_{l}_{half}", [1, 3 * 512], F32))

            dma("sync", b1_sb[:, :], b1_h.ap()[l], sem="d3")
            dma("sync", b2_sb[:, :], b2_h.ap()[l], sem="d3")
            dma("sync", lng_sb[:, :], lng_h.ap()[l], sem="d3")
            btok = dma("sync", lnb_sb[:, :], lnb_h.ap()[l], sem="d3")

            w1tok = [None] * NPAIR
            for i in range(3):
                w1tok[i] = dma("gpsimd", w1c[i][:, :],
                               kpn(w1_h, 8, 256, base=(l * NPAIR + i) * 8 * 128 * 256), sem=f"wd{i % 3}")

            ln_tok = [None, None]
            prev_stats = None
            prev_lt = None
            for bk in range(2):
                j0 = i0 + 512 * bk
                v_copy(A(hs, 0, [[4096, 128], [512, 4], [1, 512]]),
                       A(RES, 1 + j0, [[8 * RC, 128], [RC, 4], [1, 512]]),
                       [prev_stats] if prev_stats else ())
                cast2 = v_copy(A(hs, 2048, [[4096, 128], [512, 4], [1, 512]]),
                               A(RES, 4 * RC + j0, [[8 * RC, 128], [RC, 4], [1, 512]]), inc="v")
                sum_mm = None
                for k in range(8):
                    sum_mm = t_mm(PS[6][0:1, 0:512], ones_col_b[:, :], hs[:, 512 * k:512 * k + 512],
                                  k == 0, k == 7,
                                  ([cast2] + ([prev_lt] if prev_lt else [])) if k == 0 else (),
                                  inc="t" if k == 7 else None)
                sq_tok = s_act(hs[:, :], hs[:, :], AF.Square, waits=[sum_mm], inc="s")
                st_mm = None
                for k in range(8):
                    st_mm = t_mm(PS[7][0:1, 0:512], ones_col_b[:, :], hs[:, 512 * k:512 * k + 512],
                                 k == 0, k == 7, [sq_tok] if k == 0 else (),
                                 inc="t" if k == 7 else None)
                prev_stats = st_mm
                v_ts(ALU.mult, sp[0:1, 0:512], PS[6][0:1, 0:512], -1.0 / 1024.0, waits=[st_mm])
                v_ts(ALU.mult, sp[0:1, 512:1024], PS[7][0:1, 0:512], 1.0 / 1024.0)
                v_tt(ALU.mult, sp[0:1, 1024:1536], sp[0:1, 0:512], sp[0:1, 0:512])
                v_tt(ALU.subtract, sp[0:1, 512:1024], sp[0:1, 512:1024],
                     sp[0:1, 1024:1536])
                var_tok = v_ts(ALU.add, sp[0:1, 512:1024], sp[0:1, 512:1024], 1e-5, inc="v")
                lnv = s_act(sp[0:1, 512:1024], sp[0:1, 512:1024], AF.Ln, waits=[var_tok], inc="s")
                halfm = v_ts(ALU.mult, sp[0:1, 512:1024], sp[0:1, 512:1024], -0.5, waits=[lnv], inc="v")
                rstd_tok = s_act(sp[0:1, 512:1024], sp[0:1, 512:1024], AF.Exp, waits=[halfm], inc="s")
                nmr_tok = v_tt(ALU.mult, sp[0:1, 1024:1536], sp[0:1, 0:512], sp[0:1, 512:1024],
                               [rstd_tok], inc="v")
                br1 = t_mm(PS[6][:, :], ones_row_f[:, :], sp[0:1, 512:1024], True, True,
                           [rstd_tok], inc="t")
                br2 = t_mm(PS[7][:, :], ones_row_f[:, :], sp[0:1, 1024:1536], True, True,
                           [nmr_tok], inc="t")
                lt = None
                for k in range(8):
                    sh_ = 1 if k >= 4 else 0
                    t1 = v_tt(ALU.mult, tmp[:, :],
                              A(RES, RC * k + 1 + j0 - sh_, [[8 * RC, 128], [1, 512]]),
                              PS[6][:, :], [br1, br2] + ([lt] if lt else []))
                    t2_ = v_tt(ALU.add, tmp[:, :], tmp[:, :], PS[7][:, :], inc="v")
                    lt = v_stt(LN1[:, 1024 * k + 512 * bk: 1024 * k + 512 * bk + 512], tmp[:, :],
                               lng_sb[:, k:k + 1], lnb_sb[:, k:k + 1].broadcast_to([128, 512]),
                               ALU.mult, ALU.add, [t2_, btok], inc="v")
                ln_tok[bk] = lt
                prev_lt = lt

            mh_tok = None
            g_mm_last = None
            for i in range(NPAIR):
                for bk in range(2):
                    par = 2 * (i % 2)
                    w = [w1tok[i], ln_tok[bk]]
                    if (i >= 2 and bk == 0) or bk == 1:
                        w.append(mh_tok)
                    a_mm = g_mm = None
                    for k in range(8):
                        a_mm = t_mm(PS[par][:, :], w1c[i % 3][:, 256 * k: 256 * k + 128],
                                    LN1[:, 1024 * k + 512 * bk: 1024 * k + 512 * bk + 512],
                                    k == 0, k == 7, w if k == 0 else ())
                        g_mm = t_mm(PS[par + 1][:, :], w1c[i % 3][:, 256 * k + 128: 256 * k + 256],
                                    LN1[:, 1024 * k + 512 * bk: 1024 * k + 512 * bk + 512],
                                    k == 0, k == 7, (), inc="t" if k == 7 else None)
                    gelu_tok = s_act(gl[:, :], PS[par + 1][:, :], AF.Gelu,
                                     bias=b1_sb[:, 2 * i + 1:2 * i + 2],
                                     waits=[g_mm, btok] + ([mh_tok] if mh_tok else []), inc="s")
                    mh_tok = v_stt(MH[:, 1024 * i + 512 * bk: 1024 * i + 512 * bk + 512],
                                   PS[par][:, :], b1_sb[:, 2 * i:2 * i + 1], gl[:, :],
                                   ALU.add, ALU.mult, [gelu_tok], inc="v")
                g_mm_last = g_mm
                if i + 3 < NPAIR:
                    w1tok[i + 3] = dma("gpsimd", w1c[(i + 3) % 3][:, :],
                                       kpn(w1_h, 8, 256, base=(l * NPAIR + i + 3) * 8 * 128 * 256),
                                       [mh_tok], sem=f"wd{(i + 3) % 3}")

            w2tok = [None] * 8
            w2tok[0] = dma("gpsimd", w2c[0][:, :], kpn(w2_h, NPAIR, 128, base=(l * 8 + 0) * NPAIR * 128 * 128), sem="wd0")
            w2tok[1] = dma("gpsimd", w2c[1][:, :], kpn(w2_h, NPAIR, 128, base=(l * 8 + 1) * NPAIR * 128 * 128), sem="wd1")
            evtok = {4: None, 5: None}
            for oc in range(8):
                for bk in range(2):
                    par = 4 + (2 * oc + bk) % 2
                    w = [w2tok[oc], mh_tok]
                    if evtok[par]:
                        w.append(evtok[par])
                    w2_mm = None
                    for k in range(NPAIR):
                        w2_mm = t_mm(PS[par][:, :], w2c[oc % 2][:, 128 * k:128 * k + 128],
                                     MH[:, 1024 * k + 512 * bk: 1024 * k + 512 * bk + 512],
                                     k == 0, k == NPAIR - 1, w if k == 0 else (),
                                     inc="t" if k == NPAIR - 1 else None)
                    evtok[par] = v_stt(res_ap(oc, i0 + 512 * bk, 512), PS[par][:, :],
                                       b2_sb[:, oc:oc + 1], res_ap(oc, i0 + 512 * bk, 512),
                                       ALU.add, ALU.add, [w2_mm, btok], inc="v")
                if tail_oc is not None:
                    tail_oc(oc, evtok[4], evtok[5])
                if oc + 2 < 8:
                    w2tok[oc + 2] = dma("gpsimd", w2c[oc % 2][:, :],
                                        kpn(w2_h, NPAIR, 128, base=(l * 8 + oc + 2) * NPAIR * 128 * 128),
                                        [evtok[4 + (2 * oc + 1) % 2]], sem=f"wd{oc % 2}")
            em.run_block(f"mlp_{l}_{half}")

    def cc_stage_head():
        dma("sync", bass.AP(cc_in, 0, [[1024, 64], [1, 1024]]), kv_part[0:64, :], sem="d0")
        dma("sync", bass.AP(cc_in, 512 * 128, [[16, 64], [1, 16]]), ksum_part[0:64, :], sem="d0")

    def cc_stage_oc(oc, e4, e5):
        return dma("sync", bass.AP(cc_in, (520 + 640 * oc) * 128, [[640, 128], [1, 640]]),
                   res_ap(oc, TOWN, HALO), [e4, e5], sem="d0")

    def cc_launch(stage_tok):
        em.op("gpsimd", lambda g: g.collective_compute(
            "AllGather", ALU.bypass, replica_groups=[list(range(NCORES))],
            ins=[cc_in.ap().opt()], outs=[cc_out.ap().opt()]), [stage_tok], inc="cc")
        return ("cc", em.counts["cc"])

    def cc_recv(cctok):
        with contextlib.ExitStack() as ph:
            kv8 = ph.enter_context(nc.sbuf_tensor("kv8", [64, 8 * 1024], F32))
            ks8 = ph.enter_context(nc.sbuf_tensor("ks8", [64, 8 * 16], F32))
            ldts = []
            for r in range(NCORES):
                q, sm = (("sync", "d0") if r % 2 == 0 else ("scalar", "d2"))
                dma(q, kv8[0:64, 1024 * r:1024 * (r + 1)],
                    bass.AP(cc_out, CCROWS * 128 * r, [[1024, 64], [1, 1024]]), [cctok], sem=sm)
                ldts.append(dma(q, ks8[0:64, 16 * r:16 * (r + 1)],
                                bass.AP(cc_out, CCROWS * 128 * r + 512 * 128, [[16, 64], [1, 16]]),
                                (), sem=sm))
            ldt2 = ldts[-2]
            ldt = ldts[-1]

            res_dsts = [res_ap(k, 0, HALO) for k in range(8)]
            sems = em.sems

            def _halo(e, ks, dsts=tuple(res_dsts)):
                pid = e.partition_id()
                with e.register("off") as off, e.register("tmr") as tmr:
                    e.reg_alu(off, pid, NCORES - 1, ALU.add)
                    e.reg_alu(tmr, off, NCORES, ALU.divide)
                    e.reg_alu(tmr, tmr, NCORES, ALU.mult)
                    e.reg_alu(off, off, tmr, ALU.subtract)
                    e.reg_alu(off, off, CCROWS * 128, ALU.mult)
                    inst = None
                    for k in ks:
                        with e.register(f"offk{k}") as offk:
                            e.reg_alu(offk, off, (520 + 640 * k) * 128, ALU.add)
                            src = bass.AP(cc_out, offk, [[640, 128], [1, 640]])
                            inst = e.dma_start(out=dsts[k], in_=src)
                            inst.then_inc(sems["bd"], 16)
                    return inst

            em.counts["bd"] += 16 * 8
            halo_tok = ("bd", em.counts["bd"])
            em.op("scalar", lambda e: _halo(e, (0, 1, 2, 3)), [cctok])
            em.op("sync", lambda e: _halo(e, (4, 5, 6, 7)), [cctok])
            sc_tok = None
            for k in range(8):
                sc_tok = v_ts(ALU.mult, res_ap(k, 0, HALO), res_ap(k, 0, HALO), bsc_rep[:, :],
                              [halo_tok] if k == 0 else (), inc="v" if k == 7 else None)
            v_tt(ALU.add, kv_part[0:64, :], kv8[0:64, 0:1024], kv8[0:64, 1024:2048], [ldt, ldt2])
            for r in range(2, 8):
                v_tt(ALU.add, kv_part[0:64, :], kv_part[0:64, :], kv8[0:64, 1024 * r:1024 * (r + 1)])
            v_tt(ALU.add, ksum_part[0:64, :], ks8[0:64, 0:16], ks8[0:64, 16:32])
            for r in range(2, 8):
                v_tt(ALU.add, ksum_part[0:64, :], ksum_part[0:64, :], ks8[0:64, 16 * r:16 * (r + 1)])
            v_ts(ALU.add, ksum_part[0:64, :], ksum_part[0:64, :], 1e-6)
            v_copy(A(kvaug, 0, [[16 * 65, 64], [65, 16], [1, 64]]),
                   A(kv_part, 0, [[1024, 64], [64, 16], [1, 64]]))
            kva = v_copy(A(kvaug, 64, [[16 * 65, 64], [65, 16], [1, 1]]),
                         ksum_part[0:64, :].unsqueeze(2), inc="v")
            dup = dma("sync", kvaug[64:128, :], kvaug[0:64, :], [kva], sem="bd")
            em.run_block("cc_recv")
            return dup

    layer_qkv_attn(0)
    if n_phases < 4:
        return finish()
    cc_stage_head()
    _stage = [None]

    def _tail_oc(oc, e4, e5):
        _stage[0] = cc_stage_oc(oc, e4, e5)

    layer_mlp(0, 1, tail_oc=_tail_oc)
    if n_phases < 5:
        return finish()
    cctok = cc_launch(_stage[0])
    layer_mlp(0, 0)
    if n_phases < 6:
        return finish()
    kv_dup = cc_recv(cctok)
    if n_phases < 7:
        return finish()
    layer_qkv_attn(1)
    if n_phases < 8:
        return finish()
    layer_mlp(1, 1)
    layer_mlp(1, 0)
    if n_phases < 9:
        return finish()

    with contextlib.ExitStack() as ph:
        xt = [ph.enter_context(nc.sbuf_tensor(f"fxt{i}", [128, 1024], F32)) for i in range(2)]
        qt = [ph.enter_context(nc.sbuf_tensor(f"fqt{i}", [128, 1024], BF16)) for i in range(2)]
        qfm = ph.enter_context(nc.sbuf_tensor("fqfm", [128, 1024], BF16))
        OUT = [ph.enter_context(nc.sbuf_tensor(f"fout{i}", [128, 1024], F32)) for i in range(2)]
        rcp = ph.enter_context(nc.sbuf_tensor("frcp", [128, 1], F32))
        ost = [None, None]
        prev_ev = None
        prev_qcp = None
        prev_mm = None
        prev_h = None
        for i in range(16):
            j0 = HALO + 128 * i
            w = [ost[i % 2]] if ost[i % 2] else []
            xl = dma("sync", xt[i % 2][:, :], x_sl.ap()[j0:j0 + 128, :], w, sem=f"d{i % 2}")
            ql = dma("scalar", qt[i % 2][:, :], q_dram.ap()[128 * i:128 * i + 128, :], w, sem=f"d{2 + i % 2}")
            tr = None
            for k in range(8):
                wt = []
                if k == 0:
                    wt = [xl] + ([prev_ev] if prev_ev else [])
                tr = t_tr(PS[k % 2][:, 128 * (k // 2):128 * (k // 2) + 128], res_ap(k, j0, 128),
                          ident_f[:, :], wt, inc="t" if k == 7 else None)
            v_tt(ALU.add, A(OUT[i % 2], 0, [[1024, 128], [256, 4], [1, 128]]),
                 PS[0][:, 0:512], A(xt[i % 2], 0, [[1024, 128], [256, 4], [1, 128]]), [tr])
            prev_ev = v_tt(ALU.add, A(OUT[i % 2], 128, [[1024, 128], [256, 4], [1, 128]]),
                           PS[1][:, 0:512], A(xt[i % 2], 128, [[1024, 128], [256, 4], [1, 128]]),
                           inc="v")
            trq = None
            for blkn in range(8):
                wt = []
                if blkn == 0:
                    wt = [ql] + ([prev_qcp] if prev_qcp else [])
                trq = t_tr(ps_bf(2 + blkn % 2)[:, 128 * (blkn // 2):128 * (blkn // 2) + 128],
                           qt[i % 2][:, 128 * blkn:128 * blkn + 128], ident_b[:, :],
                           wt, inc="t" if blkn == 7 else None)
            wq = [trq] + ([prev_mm] if prev_mm else [])
            s_act(A(qfm, 0, [[1024, 128], [256, 4], [1, 128]]),
                  ps_bf(2)[:, 0:512], AF.Copy, waits=wq)
            prev_qcp = s_act(A(qfm, 128, [[1024, 128], [256, 4], [1, 128]]),
                             ps_bf(3)[:, 0:512], AF.Copy, inc="s")
            for h in range(16):
                p, pb = h // 2, 64 * (h % 2)
                par = 4 + h % 2
                prev_mm = t_mm(PS[par][:, 0:65], qfm[pb:pb + 64, 128 * p:128 * p + 128],
                               kvaug[pb:pb + 64, 65 * h:65 * h + 65], True, True,
                               [prev_qcp, kv_dup] + ([prev_h] if prev_h else []), inc="t")
                v_recip(rcp[:, :], PS[par][:, 64:65], waits=[prev_mm])
                prev_h = v_stt(OUT[i % 2][:, 64 * h:64 * h + 64], PS[par][:, 0:64], rcp[:, :],
                               OUT[i % 2][:, 64 * h:64 * h + 64], ALU.mult, ALU.add, inc="v")
            ost[i % 2] = dma("gpsimd", out_d.ap()[128 * i:128 * i + 128, :], OUT[i % 2][:, :],
                             [prev_h], sem=f"go{i % 2}")
        em.run_block("finale")

    return finish()


def kernel(**inputs):
    sh = _prep_shared(inputs)
    in_maps = [_prep_core(inputs, sh, c) for c in range(NCORES)]
    nc = build_program()
    res = run_bass_kernel_spmd(nc, in_maps, core_ids=list(range(NCORES)))
    out = np.concatenate([res.results[c]["out_d"] for c in range(NCORES)], 0)
    return out.reshape(1, N, D)
